# version 25
# baseline (speedup 1.0000x reference)
import numpy as np
import ml_dtypes
import concourse.bass as bass
import concourse.mybir as mybir
from concourse.bass_utils import run_bass_kernel_spmd
from concourse.alu_op_type import AluOpType

F32 = mybir.dt.float32
BF16 = mybir.dt.bfloat16
I32 = mybir.dt.int32
I8 = mybir.dt.int8
AFT = mybir.ActivationFunctionType
AX = mybir.AxisListType
OP = AluOpType

D = 1024
INNER = 2048
STATE = 64
NTOK = 4096
WARM = 768
TT = 256
NCH = TT // 128
NT_MAIN = NTOK // TT
NT_WARM = WARM // TT
KD = D // 128
MI = INNER // 128
LN_EPS = 1e-5
HALO = 3
XSLOT = TT + 8


class Stream:

    def __init__(self, key, track_hazards, dma_chain=False):
        self.key = key
        self.items = []
        self.cnt = 0
        self.track = track_hazards
        self.dma_chain = dma_chain
        self.wkey = {}
        self.rkey = {}
        self.guard = 0

    def op(self, fn, reads=(), writes=(), waits=(), inc=0):
        key = self.key
        waits = list(waits)
        if self.dma_chain and self.cnt > 0:
            waits.append((key, self.cnt))
        if self.track:
            hz = 0
            for k in set(reads) | set(writes):
                hz = max(hz, self.wkey.get(k, 0))
            for k in set(writes):
                hz = max(hz, self.rkey.get(k, 0))
            selfwait = hz if hz > self.guard else 0
            if selfwait:
                waits.append((key, selfwait))
                self.guard = selfwait
            inc = 1
            newcnt = self.cnt + 1
            for k in writes:
                self.wkey[k] = newcnt
            for k in reads:
                self.rkey[k] = newcnt
        else:
            selfwait = 0

        def run(eng, sems):
            for sk, val in waits:
                eng.wait_ge(sems[sk], val)
            ins = fn(eng)
            if inc:
                ins.then_inc(sems[key], inc)
        self.items.append(run)
        if inc:
            self.cnt += inc
        return self.cnt


def build_nc(debug=False):
    nc = bass.Bass("TRN2", target_bir_lowering=False)

    Xd = nc.dram_tensor("X", [WARM + NTOK, D], F32, kind="ExternalInput").ap()
    OUTd = nc.dram_tensor("OUT", [NTOK, D], F32, kind="ExternalOutput").ap()
    WINd = nc.dram_tensor("WIN", [2 * INNER, D], F32, kind="ExternalInput").ap()
    WOUTd = nc.dram_tensor("WOUT", [D, INNER], F32, kind="ExternalInput").ap()
    WBd = nc.dram_tensor("WB", [STATE, INNER], F32, kind="ExternalInput").ap()
    WCd = nc.dram_tensor("WC", [INNER, STATE], F32, kind="ExternalInput").ap()
    FWd = nc.dram_tensor("FW", [D, D], F32, kind="ExternalInput").ap()
    CWd = nc.dram_tensor("CW", [128, MI * 4], F32, kind="ExternalInput").ap()
    CBd = nc.dram_tensor("CB", [128, MI], F32, kind="ExternalInput").ap()
    DPd = nc.dram_tensor("DP", [128, MI], F32, kind="ExternalInput").ap()
    GAMd = nc.dram_tensor("GAM", [1, D], F32, kind="ExternalInput").ap()
    BETd = nc.dram_tensor("BET", [1, D], F32, kind="ExternalInput").ap()
    FBd = nc.dram_tensor("FB", [1, D], F32, kind="ExternalInput").ap()
    IDTd = nc.dram_tensor("IDT", [128, 128], BF16, kind="ExternalInput").ap()
    FLGd = nc.dram_tensor("FLG", [STATE, 1], F32, kind="ExternalInput").ap()
    WSH = nc.dram_tensor("WSH", [1, D], F32, kind="Internal").ap()

    from contextlib import ExitStack
    es = ExitStack()

    def sb(name, shape, dt):
        return es.enter_context(nc.sbuf_tensor(name, shape, dt))

    Wi = sb("Wi", [128, 2 * MI * KD * 128], BF16)
    Wo = sb("Wo", [128, MI * D], BF16)
    Wq = sb("Wq", [128, KD * D], BF16)
    Wb = sb("Wb", [128, MI * STATE], BF16)
    Wc = sb("Wc", [128, INNER], BF16)
    Dcv = sb("Dcv", [128, MI * 4 * 128], BF16)
    GAM = sb("GAMb", [128, D], F32)
    BET = sb("BETb", [128, D], F32)
    WSC = sb("WSCb", [128, D], F32)
    FBI = sb("FBIb", [128, D], F32)
    CB = sb("CBb", [128, MI], F32)
    DP = sb("DPb", [128, MI], F32)
    CW = sb("CWb", [128, MI * 4], F32)
    IDT = sb("IDTb", [128, 128], BF16)
    DEC = sb("DECb", [STATE, TT], F32)
    FLG = sb("FLGb", [STATE, 1], F32)
    ONE1 = sb("ONE1", [1, 128], F32)

    def alias(name, shape, dtype, backing, byte_off=0):
        h = nc.virtual_sbuf_tensor(name, shape, dtype)
        nc._tensor(h.name, list(h.shape), dtype, type="SB")
        mloc = nc.lookup_mloc(h)
        mloc.allocated = True
        mloc.addr = nc.lookup_mloc(backing).addr + byte_off
        return h

    XT = [sb("XT0", [128, NCH * D], F32), sb("XT1", [128, NCH * D], F32)]
    XN1B = sb("XN1B", [128, NCH * D], BF16)
    FM1 = sb("FM1", [128, KD * TT], BF16)
    FM2 = sb("FM2", [128, KD * TT], BF16)
    XIN = sb("XIN", [128, 2 * XSLOT], BF16)
    HLO = sb("HLO", [128, MI * 4], BF16)
    U = sb("U", [128, MI * TT], BF16)
    G = sb("G", [128, 8 * TT], BF16)
    Z = sb("Z", [128, 2 * TT], BF16)
    SCR = alias("SCRv", [128, D], F32, G)
    BU = sb("BUb", [STATE, TT], F32)
    HSB = sb("HSBb", [STATE, TT], BF16)
    HC = sb("HCb", [STATE, 1], F32)
    V2 = sb("V2b", [128, NCH * D], F32)
    XQB = sb("XQB", [128, NCH * D], BF16)
    ZT = alias("ZTv", [128, TT], F32, XN1B)
    ST = CW

    I32B = alias("I32B", [128, 1024], I32, FM2)
    I32U = alias("I32U", [128, 1024], I32, U)
    SQD = alias("SQDv", [128, 1024], F32, U, byte_off=MI * TT)
    XND = XN1B
    TQS = alias("TQSv", [128, 1024], F32, FM2)
    I8Q = alias("I8Qv", [128, 1024], I8, XN1B, byte_off=1024)
    UKEYS = tuple(f"U{j}" for j in range(16))
    def usq(c):
        return XN1B[:, D:2 * D]

    def st(col):
        return ST[:, col:col + 1]

    pb = {}
    for j in (0, 1, 2, 3, 6, 7):
        pb[j] = es.enter_context(nc.psum_tensor(f"pb{j}", [128, 512], F32))
    vb4 = es.enter_context(nc.psum_tensor("vb4", [128, 1024], BF16))
    vb5 = es.enter_context(nc.psum_tensor("vb5", [128, 1024], BF16))

    def psum_alias(name, shape, dtype, backing):
        h = nc.virtual_psum_tensor(name, shape, dtype)
        nc._tensor(h.name, list(h.shape), dtype, type="PSUM")
        mloc = nc.lookup_mloc(h)
        bm = nc.lookup_mloc(backing)
        mloc.allocated = True
        mloc.addr = bm.addr
        mloc.bank = bm.bank
        mloc.base = bm.base
        return h

    YB = [psum_alias("yb4f", [128, 512], F32, vb4),
          psum_alias("yb5f", [128, 512], F32, vb5)]
    FB = [psum_alias("fb4f", [128, 512], F32, vb4),
          psum_alias("fb5f", [128, 512], F32, vb5)]

    sems = {}
    for k in ("T", "V", "S", "Q", "G", "R"):
        sems[k] = es.enter_context(nc.semaphore("sem" + k))

    T = Stream("T", False)
    V = Stream("V", True)
    S = Stream("S", True)
    Q = Stream("Q", False, dma_chain=True)
    Gq = Stream("G", False, dma_chain=True)
    ev = {}
    lnq_def = {}
    qth_def = {}

    ev["cdma"] = Q.op(lambda q: q.dma_start(out=CW[:], in_=CWd), inc=16)
    ev["cbdma"] = Q.op(lambda q: q.dma_start(out=CB[:], in_=CBd), inc=16)
    ev["dpdma"] = Q.op(lambda q: q.dma_start(out=DP[:], in_=DPd), inc=16)
    ev["idma"] = Q.op(lambda q: q.dma_start(out=IDT[:], in_=IDTd), inc=16)
    ev["fdma"] = Q.op(lambda q: q.dma_start(out=FLG[:], in_=FLGd), inc=16)
    V.op(lambda v: v.memset(DEC[:], 0.95), writes=("DEC",))
    V.op(lambda v: v.memset(ONE1[:], 1.0), writes=("ONE1",))
    V.op(lambda v: v.memset(HC[:], 0.0), writes=("HC",))
    ev["hlomem"] = V.op(lambda v: v.memset(HLO[:], 0.0), writes=("HLO",), inc=1)
    for f in range(MI):
        for kk in range(4):
            lastd = (f == MI - 1 and kk == 3)
            ev["diag"] = V.op(
                (lambda ff, k2: lambda v: v.tensor_scalar(
                    Dcv[:, (ff * 4 + k2) * 128:(ff * 4 + k2 + 1) * 128],
                    IDT[:], CW[:, ff * 4 + k2:ff * 4 + k2 + 1], None, OP.mult))(f, kk),
                reads=("IDT", "CW"), writes=("Dcv",),
                waits=[("Q", ev["idma"])] if (f == 0 and kk == 0) else [],
                inc=1 if lastd else 0)
    V.op(lambda v: v.memset(st(44), LN_EPS), reads=("CW",), writes=("CW", "ST44"))

    def prep_win_tile(m, prev_cast, prev_tp, prev_ev):
        par = m % 2
        stage = V2[:, par * D:(par + 1) * D]
        bstage_lo = par * D
        BST = U if m < MI else XQB
        bkeys = ("U",) if m < MI else ("XQB0", "XQB1")
        ev[f"wdma{m}"] = (Q if m % 2 == 0 else Gq).op(
            (lambda ss, mm: lambda q: q.dma_start(out=ss, in_=WINd[mm * 128:(mm + 1) * 128, :]))(stage, m),
            waits=[("S", prev_cast)] if prev_cast else [], inc=16)
        ev[f"wcast{m}"] = S.op(
            (lambda ss, lo, bb: lambda s: s.activation(bb[:, lo:lo + D], ss, AFT.Copy))(stage, bstage_lo, BST),
            reads=bkeys, writes=bkeys,
            waits=[("Q" if m % 2 == 0 else "G", ev[f"wdma{m}"])] + ([("T", prev_tp)] if prev_tp else []),
            inc=1)
        vb = vb4 if par == 0 else vb5
        for k in range(KD):
            w_ = []
            if k == 0:
                w_.append(("S", ev[f"wcast{m}"]))
                if prev_ev:
                    w_.append(("S", prev_ev))
            ev[f"wtp{m}"] = T.op(
                (lambda kk, vv, lo, bb: lambda t: t.transpose(
                    vv[:, kk * 128:(kk + 1) * 128], bb[:, lo + kk * 128:lo + (kk + 1) * 128], IDT[:]))(k, vb, bstage_lo, BST),
                waits=w_, inc=1 if k == KD - 1 else 0)
        ev[f"wev{m}"] = S.op(
            (lambda vv, mm: lambda s: s.copy(
                Wi[:, mm * KD * 128:(mm + 1) * KD * 128], vv[:, 0:1024]))(vb, m),
            reads=(), writes=(),
            waits=[("T", ev[f"wtp{m}"])], inc=1)

    pc = pt = pe = None
    for m in range(MI):
        prep_win_tile(m,
                      ev.get(f"wcast{m - 2}"),
                      ev.get(f"wtp{m - 2}"),
                      ev.get(f"wev{m - 2}"))

    ev["wbdma"] = Q.op(lambda q: q.dma_start(out=V2[0:STATE, 0:INNER], in_=WBd),
                       waits=[("S", ev[f"wcast{MI - 1}"])], inc=16)
    ev["wbcast"] = S.op(lambda s: s.activation(XQB[0:STATE, 0:INNER], V2[0:STATE, 0:INNER], AFT.Copy),
                        reads=("XQB0", "XQB1"), writes=("XQB0", "XQB1"),
                        waits=[("Q", ev["wbdma"]), ("T", ev[f"wtp{MI - 1}"])], inc=1)
    for f in range(MI):
        w_ = [("S", ev["wbcast"])] if f == 0 else []
        if f == 0:
            w_.append(("S", ev[f"wev{MI - 1}"]))
        ev["wbtp"] = T.op(
            (lambda ff: lambda t: t.transpose(
                vb4[:, ff * STATE:(ff + 1) * STATE],
                XQB[0:STATE, ff * 128:(ff + 1) * 128], IDT[0:STATE, 0:STATE]))(f),
            waits=w_, inc=1 if f == MI - 1 else 0)
    ev["wbev"] = S.op(lambda s: s.copy(Wb[:], vb4[:, 0:MI * STATE]),
                      waits=[("T", ev["wbtp"])], inc=1)

    def prep_late():
        ev["wcdma"] = Q.op(
            lambda q: q.dma_start(
                out=V2[:, 0:MI * STATE].rearrange("p (f s) -> p f s", f=MI),
                in_=WCd.rearrange("(f p) s -> p f s", p=128)),
            waits=[("S", ev["wbcast"])], inc=16)
        ev["wccast"] = S.op(lambda s: s.activation(XQB[:, 0:MI * STATE], V2[:, 0:MI * STATE], AFT.Copy),
                            reads=("XQB0", "XQB1"), writes=("XQB0", "XQB1"),
                            waits=[("Q", ev["wcdma"]), ("T", ev["wbtp"])], inc=1)
        for half in range(2):
            vb = vb4 if half == 0 else vb5
            for j in range(8):
                f = half * 8 + j
                w_ = []
                if f == 0:
                    w_ += [("S", ev["wccast"]), ("S", ev["wbev"])]
                    if f"fm1ev{NT_WARM - 1}_{NCH - 1}" in ev:
                        w_.append(("S", ev[f"fm1ev{NT_WARM - 1}_{NCH - 1}"]))
                if f == 8:
                    w_ += [("S", ev["wcev0"])]
                ev["wctp"] = T.op(
                    (lambda ff, jj, vv: lambda t: t.transpose(
                        vv[0:STATE, jj * 128:(jj + 1) * 128],
                        XQB[:, ff * STATE:(ff + 1) * STATE], IDT[:]))(f, j, vb),
                    waits=w_, inc=1 if j == 7 else 0)
            ev[f"wcev{half}"] = S.op(
                (lambda vv, hh: lambda s: s.copy(
                    Wc[0:STATE, hh * 1024:(hh + 1) * 1024], vv[0:STATE, 0:1024]))(vb, half),
                waits=[("T", ev["wctp"])], inc=1)

        for m in range(MI, 2 * MI):
            prep_win_tile(m,
                          ev.get(f"wcast{m - 2}") if m >= MI + 2 else ev["wccast"],
                          ev.get(f"wtp{m - 2}") if m >= MI + 2 else ev["wctp"],
                          ev.get(f"wev{m - 2}") if m >= MI + 2 else ev["wcev1"])

        WoA = Wo[:].rearrange("p (k d) -> p k d", k=MI)
        for d8 in range(8):
            ev[f"wodma{d8}"] = (Q if d8 % 2 == 0 else Gq).op(
                (lambda dd: lambda q: q.dma_start(out=V2[:, 0:INNER],
                                                  in_=WOUTd[dd * 128:(dd + 1) * 128, :]))(d8),
                waits=[("S", ev[f"wocast{d8 - 1}"] if d8 >= 1 else ev[f"wcast{2 * MI - 1}"])],
                inc=16)
            ev[f"wocast{d8}"] = S.op(
                lambda s: s.activation(XQB[:, 0:INNER], V2[:, 0:INNER], AFT.Copy),
                reads=("XQB0", "XQB1"), writes=("XQB0", "XQB1"),
                waits=[("Q" if d8 % 2 == 0 else "G", ev[f"wodma{d8}"]),
                       ("T", ev[f"wotp{d8 - 1}"] if d8 >= 1 else ev[f"wtp{2 * MI - 1}"])],
                inc=1)
            for k in range(MI):
                vb = vb4 if k < 8 else vb5
                w_ = []
                if k == 0:
                    w_.append(("S", ev[f"wocast{d8}"]))
                    w_.append(("S", ev[f"woev{d8 - 1}"] if d8 >= 1 else ev[f"wev{2 * MI - 1}"]))
                ev[f"wotp{d8}"] = T.op(
                    (lambda kk, vv: lambda t: t.transpose(
                        vv[:, (kk % 8) * 128:((kk % 8) + 1) * 128],
                        XQB[:, kk * 128:(kk + 1) * 128], IDT[:]))(k, vb),
                    waits=w_, inc=1 if k == MI - 1 else 0)

            def woev(dd):
                def go(s):
                    s.copy(WoA[:, 0:8, dd * 128:(dd + 1) * 128], vb4[:, 0:1024])
                    return s.copy(WoA[:, 8:16, dd * 128:(dd + 1) * 128], vb5[:, 0:1024])
                return go
            ev[f"woev{d8}"] = S.op(woev(d8), waits=[("T", ev[f"wotp{d8}"])], inc=1)

        WqA = Wq[:].rearrange("p (k d) -> p k d", k=KD)
        for t8 in range(8):
            par = t8 % 2
            vstg = V2[:, par * D:(par + 1) * D]
            TB = SCR if par == 0 else TQS
            KB = I32B if par == 0 else I32U
            QB = XQB if par == 0 else FM1
            VBT = vb4 if par == 0 else vb5
            cR, cW, cN = (50, 51, 45) if par == 0 else (46, 47, 48)
            tkey = ("SCR0", "SCR1") if par == 0 else ("TQS",)
            kkey = ("FM2i",) if par == 0 else ("I32U",) + UKEYS
            qkey = ("XQB0", "XQB1") if par == 0 else ("FM1_0", "FM1_1")
            qk = ("Q", "G")[par]
            ev[f"fqdma{t8}"] = (Q if par == 0 else Gq).op(
                (lambda tt, ss: lambda q: q.dma_start(out=ss,
                                                      in_=FWd[tt * 128:(tt + 1) * 128, :]))(t8, vstg),
                waits=[("S", ev[f"fqt{t8 - 2}"] if t8 >= 2 else ev["wocast7"])] +
                      ([("V", ev[f"fqred{t8 - 2}"])] if t8 >= 2 else []),
                inc=16)
            ev[f"fqred{t8}"] = V.op(
                (lambda ss, cc: lambda v: v.reduce_max(
                    st(cc), ss, axis=AX.X, apply_absolute_value=True))(vstg, cR),
                reads=(), writes=(f"ST{cR}",),
                waits=[(qk, ev[f"fqdma{t8}"])], inc=1)
            V.op((lambda a, b: lambda v: v.tensor_scalar(st(b), st(a), 1e-8, 1.0 / 127.0, OP.max, OP.mult))(cR, cW),
                 reads=(f"ST{cR}",), writes=(f"ST{cW}",),
                 waits=[("Q", ev[f"fqws{t8 - 2}"])] if t8 >= 2 else [])
            V.op((lambda a, b: lambda v: v.reciprocal(st(a), st(b)))(cR, cW),
                 reads=(f"ST{cW}",), writes=(f"ST{cR}",))
            ev[f"fqrec{t8}"] = V.op(
                (lambda a, b: lambda v: v.tensor_scalar(st(b), st(a), -1.0, None, OP.mult))(cR, cN),
                reads=(f"ST{cR}",), writes=(f"ST{cN}",), inc=1)
            ev[f"fqt{t8}"] = S.op(
                (lambda tb, ss, cc: lambda s: s.activation(tb[:], ss, AFT.Identity, scale=st(cc)))(TB, vstg, cN),
                reads=tkey, writes=tkey + (("G0", "G1", "G2", "G3", "G4", "G5", "G6", "G7") if par == 0 else ()),
                waits=[("V", ev[f"fqrec{t8}"])] +
                      ([("V", ev[f"fqq{t8 - 2}"])] if t8 >= 2 else
                       [("T", ev["wotp7"])]),
                inc=1)
            ev[f"fqk{t8}"] = S.op(
                (lambda kb, tb: lambda s: s.copy(kb[:, 0:D], tb[:]))(KB, TB),
                reads=tkey + kkey, writes=kkey, inc=1)
            ev[f"fqk2{t8}"] = S.op(
                (lambda qb, kb: lambda s: s.copy(qb[:, 0:D], kb[:, 0:D]))(QB, KB),
                reads=kkey, writes=qkey[:1], inc=1)
            ev[f"fqr{t8}"] = V.op(
                (lambda tb, qb: lambda v: v.tensor_tensor(tb[:], tb[:], qb[:, 0:D], OP.subtract))(TB, QB),
                reads=tkey + qkey[:1], writes=tkey,
                waits=[("S", ev[f"fqk2{t8}"])], inc=1)
            V.op((lambda tb, qb: lambda v: v.scalar_tensor_tensor(
                qb[:, 0:D], tb[:], 0.5, qb[:, 0:D], OP.is_ge, OP.add))(TB, QB),
                 reads=tkey + qkey[:1], writes=qkey[:1])
            ev[f"fqq{t8}"] = V.op(
                (lambda tb, qb: lambda v: v.scalar_tensor_tensor(
                    qb[:, D:2 * D], tb[:], -0.5, qb[:, 0:D], OP.is_le, OP.subtract))(TB, QB),
                reads=tkey + qkey, writes=qkey[1:], inc=1)
            for k in range(KD):
                w_ = []
                if k == 0:
                    w_.append(("V", ev[f"fqq{t8}"]))
                    w_.append(("S", ev[f"fqev{t8 - 2}"] if t8 >= 2 else ev["woev7"]))
                ev[f"fqtp{t8}"] = T.op(
                    (lambda kk, vv, qb: lambda t: t.transpose(
                        vv[:, kk * 128:(kk + 1) * 128],
                        qb[:, D + kk * 128:D + (kk + 1) * 128], IDT[:]))(k, VBT, QB),
                    waits=w_, inc=1 if k == KD - 1 else 0)
            ev[f"fqev{t8}"] = S.op(
                (lambda tt, vv: lambda s: s.copy(WqA[:, 0:KD, tt * 128:(tt + 1) * 128],
                                                 vv[:, 0:1024]))(t8, VBT),
                waits=[("T", ev[f"fqtp{t8}"])], inc=1)
            ev[f"fqws{t8}"] = Q.op(
                (lambda tt, cc: lambda q: q.dma_start(out=WSH[0:1, tt * 128:(tt + 1) * 128],
                                                      in_=st(cc)))(t8, cW),
                waits=[("V", ev[f"fqrec{t8}"])],
                inc=16)


        ev["wsrow"] = Q.op(lambda q: q.dma_start(out=V2[0:1, 0:D], in_=WSH),
                           waits=[("S", ev["fqt7"]), ("Q", ev["fqws7"])], inc=16)
        ev["gamdma"] = Q.op(lambda q: q.dma_start(out=V2[0:1, D:2 * D], in_=GAMd), inc=16)
        ev["negws"] = V.op(
            lambda v: v.tensor_copy(st(45), st(45)),
            reads=("ST45",), writes=("ST45",),
            waits=[("Q", ev["wsrow"])], inc=1)
        bco = [("WSC", WSC, V2[0:1, 0:D], "wsrow"), ("GAM", GAM, V2[0:1, D:2 * D], "gamdma"),
               ("BET", BET, V2[0:1, 0:D], "betdma"), ("FBI", FBI, V2[0:1, D:2 * D], "fbdma")]
        for bi, (nm, dst, row, dmae) in enumerate(bco):
            if nm == "BET":
                ev["betdma"] = Q.op(lambda q: q.dma_start(out=V2[0:1, 0:D], in_=BETd),
                                    waits=[("T", ev["bcWSC"])], inc=16)
            if nm == "FBI":
                ev["fbdma"] = Q.op(lambda q: q.dma_start(out=V2[0:1, D:2 * D], in_=FBd),
                                   waits=[("T", ev["bcGAM"])], inc=16)
            for h in range(2):
                w_ = []
                if h == 0:
                    w_.append(("Q", ev[dmae]))
                    if bi == 0:
                        w_ += [("V", ev["negws"]), ("S", ev["fqev7"])]
                    else:
                        w_.append(("S", ev[f"bcev{bi - 1}"]))
                ev[f"bc{nm}"] = T.op(
                    (lambda rr, hh: lambda t: t.matmul(
                        pb[6][:, 0:512] if hh == 0 else pb[7][:, 0:512],
                        ONE1[:], rr[0:1, hh * 512:(hh + 1) * 512],
                        start=True, stop=True))(row, h),
                    waits=w_, inc=1 if h == 1 else 0)

            def bcev(dd):
                def go(s):
                    s.copy(dd[:, 0:512], pb[6][:, 0:512])
                    return s.copy(dd[:, 512:1024], pb[7][:, 0:512])
                return go
            ev[f"bcev{bi}"] = S.op(bcev(dst), waits=[("T", ev[f"bc{nm}"])], inc=1)
        ev["prepS"] = ev["bcev3"]
        ev["prepV"] = ev["negws"]


    NTILES = NT_WARM + NT_MAIN

    def p2(col):
        return ST[:, col:col + 2]

    def newton_rsqrt_thunks(vcol, outcol, n_iter=6, waits0=()):
        th = [(180, (lambda a, b, w: lambda: V.op(
                  (lambda: lambda v: v.tensor_scalar(p2(b), p2(a), 1.0, 0.5, OP.add, OP.mult))(),
                  reads=(f"P{a}",), writes=("P62",), waits=list(w)))(vcol, 62, waits0)),
              (180, (lambda b: lambda: V.op(
                  lambda v: v.reciprocal(p2(b), p2(62)),
                  reads=("P62",), writes=(f"P{b}",)))(outcol))]
        for _ in range(n_iter):
            th += [
                (180, (lambda b: lambda: V.op(
                    (lambda: lambda v: v.tensor_tensor(p2(62), p2(b), p2(b), OP.mult))(),
                    reads=(f"P{b}",), writes=("P62",)))(outcol)),
                (180, (lambda a: lambda: V.op(
                    (lambda: lambda v: v.scalar_tensor_tensor(
                        p2(62), p2(62), -0.5, p2(a), OP.mult, OP.mult))(),
                    reads=("P62", f"P{a}"), writes=("P62",)))(vcol)),
                (180, (lambda b: lambda: V.op(
                    (lambda: lambda v: v.scalar_tensor_tensor(
                        p2(b), p2(62), 1.5, p2(b), OP.add, OP.mult))(),
                    reads=("P62", f"P{b}"), writes=(f"P{b}",)))(outcol))]
        return th

    def newton_rsqrt(vcol, outcol, n_iter=6, waits0=()):
        for _, t in newton_rsqrt_thunks(vcol, outcol, n_iter, waits0):
            t()

    def emit_scan(i):
        warm = i < NT_WARM
        wsc = [("S", ev[f"buev{i}"])]
        if f"hsb{i - 1}" in ev:
            wsc.append(("S", ev[f"hsb{i - 1}"]))
        ev[f"scan{i}"] = V.op(
            lambda v: v.tensor_tensor_scan(BU[:], DEC[:], BU[:], HC[:], OP.mult, OP.add),
            reads=("DEC", "BU", "HC"), writes=("BU",),
            waits=wsc, inc=1)
        ev[f"hcev{i}"] = V.op(lambda v: v.tensor_copy(HC[:], BU[:, TT - 1:TT]),
                              reads=("BU", "HC"), writes=("HC",), inc=1)
        if i == NT_WARM - 1:
            V.op(lambda v: v.tensor_tensor(HC[:], HC[:], FLG[:], OP.mult),
                 reads=("HC", "FLG"), writes=("HC",),
                 waits=[("Q", ev["fdma"])])
        if not warm:
            wh = [("V", ev[f"scan{i}"])]
            if f"y{i - 1}_{MI - 1}" in ev:
                wh.append(("T", ev[f"y{i - 1}_{MI - 1}"]))
            ev[f"hsb{i}"] = S.op(
                lambda s: s.copy(HSB[:], BU[:]),
                reads=(), writes=("HSB",),
                waits=wh, inc=1)

    def emit_xdma(j):
        if j >= NTILES:
            return
        pj = j % 2
        for c in range(NCH):
            r0 = j * TT + c * 128
            w_ = []
            if c == 0:
                if j >= 2:
                    w_ += [("V", ev[f"xtfreeV{j - 2}"]), ("S", ev[f"xtfreeS{j - 2}"])]
            ev[f"xdma{j}"] = Q.op(
                (lambda pp, cc, rr: lambda q: q.dma_start(
                    out=XT[pp][:, cc * D:(cc + 1) * D], in_=Xd[rr:rr + 128, :]))(pj, c, r0),
                waits=w_, inc=16)

    def stats_S_pre(i):
        p = i % 2
        for c in range(NCH):
            wsq = [("Q", ev[f"xdma{i}"])]
            if c == 0 and i == 0:
                wsq.append(("V", ev["diag"]))
            if c == 0 and i >= 1:
                wsq.append(("T", ev[f"fm1T{i - 1}_{NCH - 1}"]))
            if c == 0 and f"sq3{i - 3}_{NCH - 1}" in ev:
                wsq.append(("V", ev[f"sq3{i - 3}_{NCH - 1}"]))
            ev[f"sx{i}_{c}"] = S.op(
                (lambda pp, cc: lambda s: s.activation(
                    usq(cc), XT[pp][:, cc * D:(cc + 1) * D],
                    AFT.Identity, accum_out=st(0 + cc)))(p, c),
                reads=("XNDMP",), writes=("XNDMP", f"ST0{c}"),
                waits=wsq, inc=1)
            ev[f"sxx{i}_{c}"] = S.op(
                (lambda pp, cc: lambda s: s.activation(
                    usq(cc), XT[pp][:, cc * D:(cc + 1) * D],
                    AFT.Square, accum_out=st(2 + cc)))(p, c),
                reads=("XNDMP",), writes=("XNDMP", f"ST2{c}"),
                inc=1)

    def stats_V_thunks(i):
        p = i % 2
        EP = 12 + 2 * (i % 2)
        SG = 8 + 2 * (i % 2)
        th = []
        for c in range(NCH):
            th.append((200, (lambda cc: lambda: V.op(
                (lambda: lambda v: v.tensor_scalar(st(4 + cc), st(0 + cc), 1.0 / D, None, OP.mult))(),
                reads=(f"ST0{cc}",), writes=(f"ST4{cc}",),
                waits=[("S", ev[f"sx{i}_{cc}"])]))(c)))
            th.append((200, (lambda cc: lambda: V.op(
                (lambda: lambda v: v.tensor_scalar(st(38 + cc), st(2 + cc), 1.0 / D, None, OP.mult))(),
                reads=(f"ST2{cc}",), writes=(f"ST38{cc}",),
                waits=[("S", ev[f"sxx{i}_{cc}"])]))(c)))
            th.append((200, (lambda cc: lambda: V.op(
                (lambda: lambda v: v.tensor_tensor(st(42 + cc), st(4 + cc), st(4 + cc), OP.mult))(),
                reads=(f"ST4{cc}",), writes=(f"ST42{cc}",)))(c)))
            th.append((200, (lambda cc: lambda: V.op(
                (lambda: lambda v: v.tensor_tensor(st(38 + cc), st(38 + cc), st(42 + cc), OP.subtract))(),
                reads=(f"ST38{cc}", f"ST42{cc}"), writes=(f"ST38{cc}",)))(c)))
            th.append((200, (lambda cc: lambda: ev.__setitem__(f"vpe1{i}_{cc}", V.op(
                (lambda: lambda v: v.tensor_scalar(st(40 + cc), st(38 + cc), st(44), None, OP.add))(),
                reads=(f"ST38{cc}", "ST44"), writes=(f"ST40{cc}", "P40"), inc=1)))(c)))

        def nwt():
            w0 = [("V", ev[f"vpe1{i}_{c}"]) for c in range(NCH)]
            if f"xn1{i - 1}_{NCH - 1}" in ev:
                w0.append(("S", ev[f"xn1{i - 1}_{NCH - 1}"]))
            return newton_rsqrt_thunks(40, 6, waits0=w0)
        nbox = []
        def nstep(k):
            def go():
                if not nbox:
                    nbox.append(nwt())
                nbox[0][k][1]()
            return go
        for k in range(2 + 6 * 3):
            th.append((180, nstep(k)))
        th.append((200, (lambda sg: lambda: V.op(
            (lambda: lambda v: v.tensor_tensor(p2(sg), p2(40), p2(6), OP.mult))(),
            reads=("ST400", "ST401", "P6"),
            writes=(f"ST{sg}0", f"ST{sg}1")))(SG)))
        th.append((200, lambda: V.op(
            lambda v: v.tensor_tensor(p2(60), p2(4), p2(6), OP.mult),
            reads=("ST40", "ST41", "P6"), writes=("ST600", "ST601"))))
        th.append((200, lambda: V.op(
            lambda v: v.tensor_scalar(p2(60), p2(60), -1.0, None, OP.mult),
            reads=("ST600", "ST601"), writes=("ST600", "ST601"))))
        th.append((200, (lambda ee: lambda: ev.__setitem__(f"cf1{i}", V.op(
            (lambda: lambda v: v.tensor_scalar(p2(ee), p2(40), LN_EPS, None, OP.mult))(),
            reads=("ST400", "ST401"),
            writes=(f"ST{ee}0", f"ST{ee}1", f"EPP{ee}"), inc=1)))(EP)))
        return th

    def stats_S_post(i):
        p = i % 2
        for c in range(NCH):
            w_ = [("V", ev[f"cf1{i}"])]
            if c == 0 and f"zlast{i - 1}" in ev:
                w_.append(("V", ev[f"zlast{i - 1}"]))
            ev[f"xn1{i}_{c}"] = S.op(
                (lambda pp, cc: lambda s: s.activation(
                    XN1B[:, cc * D:(cc + 1) * D], XT[pp][:, cc * D:(cc + 1) * D],
                    AFT.Identity, bias=st(60 + cc), scale=st(6 + cc)))(p, c),
                reads=(), writes=(f"XN1B_sq{c}",),
                waits=w_, inc=1)
        ev[f"xtfreeS{i}"] = ev[f"xn1{i}_{NCH - 1}"]

    def mixer_stats(i):
        if i == 0:
            emit_xdma(0)
        stats_S_pre(i)
        for _, t in stats_V_thunks(i):
            t()
        stats_S_post(i)

    def mixer_main(i):
        p = i % 2
        warm = i < NT_WARM
        if i == 0:
            emit_xdma(1)
        FM1A = FM1[:].rearrange("p (k t) -> p k t", k=KD)
        for c in range(NCH):
            vb = vb4 if c == 0 else vb5
            for k in range(KD):
                w_ = []
                if k == 0:
                    w_.append(("S", ev[f"xn1{i}_{c}"]))
                    if f"fm2ev{i - 2}_{c}" in ev:
                        w_.append(("S", ev[f"fm2ev{i - 2}_{c}"]))
                    if i == 0 and c == 0:
                        w_.append(("S", ev["wbev"]))
                    if i == NT_WARM and c == 0:
                        w_.append(("S", ev["fqev7"]))
                ev[f"fm1T{i}_{c}"] = T.op(
                    (lambda cc, kk, vv: lambda t: t.transpose(
                        vv[:, kk * 128:(kk + 1) * 128],
                        XN1B[:, cc * D + kk * 128:cc * D + (kk + 1) * 128], IDT[:]))(c, k, vb),
                    waits=w_, inc=1 if k == KD - 1 else 0)
            ev[f"fm1ev{i}_{c}"] = S.op(
                (lambda cc, vv: lambda s: s.copy(
                    FM1A[:, 0:KD, cc * 128:(cc + 1) * 128], vv[:, 0:1024]))(c, vb),
                reads=(), writes=(f"FM1_{c}",),
                waits=[("T", ev[f"fm1T{i}_{c}"])], inc=1)

        WB_BANKS = (0, 1, 7)
        for f in range(MI + 4):
          if 2 <= f < MI + 2:
            fc = f - 2
            cbank = pb[2 + fc % 2]
            xloc = (fc % 2) * XSLOT
            for kk in range(4):
                w3 = []
                if kk == 0:
                    if i == 0 and fc == 0:
                        w3.append(("V", ev["diag"]))
                    w3.append(("S", ev[f"xinev{i}_{fc}"]))
                    if fc >= 2:
                        w3.append(("S", ev[f"uev{i}_{fc - 2}"]))
                    elif i >= 1:
                        w3.append(("S", ev[f"uev{i - 1}_{MI - 2 + fc}"]))
                    if fc < 2 and f"v2{i - 1}_1" in ev:
                        w3.append(("V", ev[f"v2{i - 1}_1"]))
                ev[f"cv{i}_{fc}"] = T.op(
                    (lambda ff, k2, cb, lo: lambda t: t.matmul(
                        cb[:, 0:TT], Dcv[:, (ff * 4 + k2) * 128:(ff * 4 + k2 + 1) * 128],
                        XIN[:, lo + k2:lo + k2 + TT],
                        start=(k2 == 0), stop=(k2 == 3)))(fc, kk, cbank, xloc),
                    waits=w3, inc=1 if kk == 3 else 0)
          if f < MI:
            bank = pb[WB_BANKS[f % 3]]
            w_ = []
            if f == 0:
                w_.append(("S", ev[f"fm1ev{i}_{NCH - 1}"]))
            if f < 2 and f"o1{i - 2}_1_{f}" in ev:
                w_.append(("V", ev[f"o1{i - 2}_1_{f}"]))
            if f == 2 and i == NT_WARM:
                w_.append(("S", ev["prepS"]))
            w_.append(("S", ev[f"wev{f}"]))
            if f >= 3:
                w_.append(("S", ev[f"xinev{i}_{f - 3}"]))
            elif i >= 1:
                w_.append(("S", ev[f"xinev{i - 1}_{MI - 1}"]))
            if f == 2 and f"v2{i - 1}_0" in ev:
                w_.append(("V", ev[f"v2{i - 1}_0"]))
            for k in range(KD):
                ev[f"xp{i}_{f}"] = T.op(
                    (lambda ff, kk, bb: lambda t: t.matmul(
                        bb[:, 0:TT], Wi[:, (ff * KD + kk) * 128:(ff * KD + kk + 1) * 128],
                        FM1[:, kk * TT:(kk + 1) * TT],
                        start=(kk == 0), stop=(kk == KD - 1)))(f, k, bank),
                    waits=w_ if k == 0 else [], inc=1 if k == KD - 1 else 0)
          if f >= 4:
            fb = f - 4
            w4 = [("S", ev[f"uev{i}_{fb}"])]
            if fb == 0:
                w4.append(("S", ev["wbev"]))
                if i == NT_WARM:
                    w4.append(("S", ev["prepS"]))
                if f"v2{i - 1}_0" in ev:
                    w4.append(("V", ev[f"v2{i - 1}_0"]))
                if f"buev{i - 1}" in ev:
                    w4.append(("S", ev[f"buev{i - 1}"]))
            ev[f"bu{i}_{fb}"] = T.op(
                (lambda ff: lambda t: t.matmul(
                    pb[6][0:STATE, 0:TT], Wb[:, ff * STATE:(ff + 1) * STATE],
                    U[:, ff * TT:(ff + 1) * TT],
                    start=(ff == 0), stop=(ff == MI - 1)))(fb),
                waits=w4, inc=1 if fb == MI - 1 else 0)
          if f < MI:
            xlo = (f % 2) * XSLOT
            w2 = [("T", ev[f"xp{i}_{f}"])]
            if f >= 2:
                w2.append(("T", ev[f"cv{i}_{f - 2}"]))
            elif i >= 1:
                w2.append(("T", ev[f"cv{i - 1}_{MI - 2 + f}"]))
            if i == 0 and f < 2:
                w2.append(("V", ev["hlomem"]))

            S.op((lambda ff, lo: lambda s: s.copy(
                    XIN[:, lo:lo + HALO], HLO[:, ff * 4:ff * 4 + HALO]))(f, xlo),
                 reads=(f"HLO{f}",), writes=(f"XINh{f % 2}",),
                 waits=w2[1:] if f >= 2 or i >= 1 else w2)
            S.op((lambda lo, bb: lambda s: s.copy(
                    XIN[:, lo + HALO:lo + HALO + TT], bb[:, 0:TT]))(xlo, bank),
                 reads=(), writes=(f"XIN{f % 2}",),
                 waits=[("T", ev[f"xp{i}_{f}"])])
            ev[f"xinev{i}_{f}"] = S.op(
                (lambda ff, lo: lambda s: s.copy(
                    HLO[:, ff * 4:ff * 4 + HALO], XIN[:, lo + TT:lo + TT + HALO]))(f, xlo),
                reads=(f"XIN{f % 2}",), writes=(f"HLO{f}",), inc=1)
          if f == 8 and warm and i + 1 < NTILES_ALL:
            mixer_stats(i + 1)
          if 2 <= f < MI + 2:
            fu = f - 2
            cbank = pb[2 + fu % 2]
            wu = [("T", ev[f"cv{i}_{fu}"]), ("Q", ev["cbdma"])]
            if i == 0 and fu == 0:
                wu.append(("S", ev[f"wcast{MI - 1}"]))
            if fu == 8 and f"sq2{i - 1}_{NCH - 1}" in ev:
                wu.append(("V", ev[f"sq2{i - 1}_{NCH - 1}"]))
            ev[f"uev{i}_{fu}"] = S.op(
                (lambda ff, cb: lambda s: s.activation(
                    U[:, ff * TT:(ff + 1) * TT], cb[:, 0:TT], AFT.Silu,
                    bias=CB[:, ff:ff + 1]))(fu, cbank),
                reads=(), writes=(f"U{fu}",),
                waits=wu, inc=1)
        wbu = [("T", ev[f"bu{i}_{MI - 1}"])]
        if f"hcev{i - 1}" in ev:
            wbu.append(("V", ev[f"hcev{i - 1}"]))
        if f"hsb{i - 1}" in ev:
            wbu.append(("S", ev[f"hsb{i - 1}"]))
        ev[f"buev{i}"] = S.op(
            lambda s: s.activation(BU[:], pb[6][0:STATE, 0:TT], AFT.Copy, scale=0.05),
            reads=(), writes=("BU",),
            waits=wbu, inc=1)
        if warm:
            emit_scan(i)
            ev[f"xtfreeV{i}"] = ev["hlomem"]
            emit_xdma(i + 2)

    def gate_block(i):
        p = i % 2
        emit_scan(i)
        sth = qth_def.pop(i - 1, []) + lnq_def.pop(i - 1, [])

        def xpg(f):
            bank = pb[f % 2]
            w_ = [("S", ev[f"wev{MI + f}"])]
            if f < 2:
                w_.append(("S", ev[f"xinev{i}_{MI - 1}"]))
            else:
                w_.append(("S", ev[f"gev{i}_{f - 2}"]))
            for k in range(KD):
                ev[f"xpg{i}_{f}"] = T.op(
                    (lambda ff, kk, bb: lambda t: t.matmul(
                        bb[:, 0:TT],
                        Wi[:, ((MI + ff) * KD + kk) * 128:((MI + ff) * KD + kk + 1) * 128],
                        FM1[:, kk * TT:(kk + 1) * TT],
                        start=(kk == 0), stop=(kk == KD - 1)))(f, k, bank),
                    waits=w_ if k == 0 else [], inc=1 if k == KD - 1 else 0)

        def gev(f):
            bank = pb[f % 2]
            wg = [("T", ev[f"xpg{i}_{f}"])]
            if f >= 8:
                wg.append(("V", ev[f"z{i}_{f - 8}"]))
            if f == 0:
                wg.append(("V", ev[f"epi{i - 2}_1"] if f"epi{i - 2}_1" in ev
                           else ev["prepV"]))
            ev[f"gev{i}_{f}"] = S.op(
                (lambda ff, bb: lambda s: s.activation(
                    G[:, (ff % 8) * TT:(ff % 8 + 1) * TT], bb[:, 0:TT], AFT.Silu))(f, bank),
                reads=("G",), writes=("G",),
                waits=wg, inc=1)

        xpg(0); gev(0); xpg(1); gev(1)
        if i + 1 < NTILES_ALL:
            stats_S_pre(i + 1)
            sth = sth + stats_V_thunks(i + 1)
        for f in range(MI):
            ybank = YB[f % 2]
            w_ = [("S", ev[f"gev{i}_{f}"])]
            if f == 0:
                w_ += [("S", ev[f"hsb{i}"]), ("S", ev["wcev1"])]
                w_.append(("S", ev[f"fm1ev{i}_{NCH - 1}"]))
            if f >= 2:
                w_.append(("V", ev[f"zt{i}_{f - 2}"]))
            ev[f"y{i}_{f}"] = T.op(
                (lambda ff, yb: lambda t: t.matmul(
                    yb[:, 0:TT], Wc[0:STATE, ff * 128:(ff + 1) * 128], HSB[:],
                    start=True, stop=True))(f, yb=ybank),
                waits=w_, inc=1)
            w5 = [("T", ev[f"y{i}_{f}"]), ("S", ev[f"gev{i}_{f}"])]
            if i == NT_WARM and f == 0:
                w5.append(("Q", ev["dpdma"]))
            if f == 0:
                w5.append(("T", ev[f"fm1T{i}_{NCH - 1}"]))
            ev[f"zt{i}_{f}"] = V.op(
                (lambda ff, yb: lambda v: v.scalar_tensor_tensor(
                    ZT[:, 0:TT], U[:, ff * TT:(ff + 1) * TT], DP[:, ff:ff + 1],
                    yb[:, 0:TT], OP.mult, OP.add))(f, ybank),
                reads=("ZT",), writes=("ZT",),
                waits=w5, inc=1)
            wz = []
            if f >= 2:
                wz.append(("T", ev[f"wo{i}_{f - 2}"]))
            ev[f"z{i}_{f}"] = V.op(
                (lambda ff: lambda v: v.tensor_tensor(
                    Z[:, (ff % 2) * TT:(ff % 2 + 1) * TT], ZT[:, 0:TT],
                    G[:, (ff % 8) * TT:(ff % 8 + 1) * TT], OP.mult))(f),
                reads=("ZT", "Z"), writes=("Z",), waits=wz, inc=1)
            if f + 2 < MI:
                xpg(f + 2); gev(f + 2)
            if f >= 1:
                budget = 1500
                while sth:
                    c0, t = sth[0]
                    if budget < 1500 and c0 > budget:
                        break
                    t()
                    budget -= c0
                    sth = sth[1:]
            w6 = [("V", ev[f"z{i}_{f}"])]
            if f == 0:
                if i == NT_WARM:
                    w6.append(("S", ev["prepS"]))
                w6.append(("T", ev[f"cv{i}_{MI - 1}"]))
                w6.append(("S", ev[f"uev{i}_{MI - 1}"]))
                w6.append(("S", ev[f"buev{i}"]))
                if f"v2{i - 1}_0" in ev:
                    w6.append(("V", ev[f"v2{i - 1}_0"]))
            for c in range(NCH):
                banks = (pb[6], pb[7]) if c == 0 else (pb[2], pb[3])
                for h in range(2):
                    ev[f"wo{i}_{f}"] = T.op(
                        (lambda ff, cc, hh, bk: lambda t: t.matmul(
                            bk[:, 0:512],
                            Z[:, (ff % 2) * TT + cc * 128:(ff % 2) * TT + (cc + 1) * 128],
                            Wo[:, ff * D + hh * 512:ff * D + (hh + 1) * 512],
                            start=(ff == 0), stop=(ff == MI - 1)))(f, c, h, banks[h]),
                        waits=w6 if (c == 0 and h == 0) else [],
                        inc=1 if (c == NCH - 1 and h == 1) else 0)
        for _, t in sth:
            t()
        ev[f"zlast{i}"] = ev[f"z{i}_{MI - 1}"]

    def v2_block(i):
        p = i % 2
        for c in range(NCH):
            banks = (pb[6], pb[7]) if c == 0 else (pb[2], pb[3])
            for h in range(2):
                w_ = [("T", ev[f"wo{i}_{MI - 1}"])]
                if c == 0 and h == 0:
                    if f"odma{i - 1}" in ev:
                        w_.append(("G", ev[f"odma{i - 1}"]))
                    if f"odmaq{i - 1}" in ev:
                        w_.append(("Q", ev[f"odmaq{i - 1}"]))
                    if i == NT_WARM:
                        w_.append(("S", ev["prepS"]))
                ev[f"v2{i}_{c}"] = V.op(
                    (lambda cc, hh, bk, pp, sg: lambda v: v.scalar_tensor_tensor(
                        V2[:, cc * D + hh * 512:cc * D + (hh + 1) * 512],
                        bk[:, 0:512], st(sg + cc),
                        XT[pp][:, cc * D + hh * 512:cc * D + (hh + 1) * 512],
                        OP.mult, OP.add,
                        accum_out=st(46 + 2 * cc + hh)))(c, h, banks[h], p, 8 + 2 * (i % 2)),
                    reads=(f"ST{8 + 2 * (i % 2)}{c}", f"XT{i % 2}_{c}"),
                    writes=(f"V2_{c}_{h}", f"ST46_{2 * c + h}"),
                    waits=w_, inc=1)

    def ln_block(i):
        p = i % 2
        EP = 12 + 2 * (i % 2)

        for c in range(NCH):
            w_ = [("V", ev[f"v2{i}_{c}"])]
            if i == NT_WARM and c == 0:
                w_.append(("S", ev["prepS"]))
            if c == 0:
                w_.append(("V", ev[f"zlast{i}"]))
            ev[f"sq2{i}_{c}"] = V.op(
                (lambda cc: lambda v: v.scalar_tensor_tensor(
                    SQD[:], V2[:, cc * D:(cc + 1) * D], 1.0,
                    V2[:, cc * D:(cc + 1) * D], OP.mult, OP.mult,
                    accum_out=st(54 + cc)))(c),
                reads=(f"V2_{c}_0", f"V2_{c}_1", "SQD"),
                writes=("SQD", "P54"),
                waits=w_, inc=1)
            V.op((lambda cc: lambda v: v.tensor_tensor(
                st(58 + cc), st(46 + 2 * cc), st(47 + 2 * cc), OP.add))(c),
                 reads=(f"ST46_{2 * c}", f"ST46_{2 * c + 1}"), writes=("P58",))
        V.op(lambda v: v.tensor_scalar(p2(58), p2(58), 1.0 / D, None, OP.mult),
             reads=("P58",), writes=("P58",))
        V.op(lambda v: v.tensor_scalar(p2(54), p2(54), 1.0 / D, None, OP.mult),
             reads=("P54",), writes=("P54",))
        V.op(lambda v: v.tensor_tensor(p2(38), p2(58), p2(58), OP.mult),
             reads=("P58",), writes=("P38",))
        V.op(lambda v: v.tensor_tensor(p2(54), p2(54), p2(38), OP.subtract),
             reads=("P54", "P38"), writes=("P54",))
        V.op((lambda ee: lambda v: v.tensor_tensor(p2(54), p2(54), p2(ee), OP.add))(EP),
             reads=("P54", f"EPP{EP}"), writes=("P54",))
        newton_rsqrt(54, 16)
        V.op(lambda v: v.tensor_tensor(p2(18), p2(58), p2(16), OP.mult),
             reads=("P58", "P16"), writes=("P18",))
        ev[f"cf2{i}"] = V.op(
            lambda v: v.tensor_scalar(p2(18), p2(18), -1.0, None, OP.mult),
            reads=("P18",), writes=("P18",), inc=1)
        for c in range(NCH):
            ev[f"n2{i}_{c}"] = V.op(
                (lambda cc: lambda v: v.tensor_scalar(
                    V2[:, cc * D:(cc + 1) * D], V2[:, cc * D:(cc + 1) * D],
                    st(16 + cc), st(18 + cc), OP.mult, OP.add))(c),
                reads=("P16", "P18", f"V2_{c}_0", f"V2_{c}_1"),
                writes=(f"V2_{c}_0", f"V2_{c}_1"), inc=1)
            ev[f"t{i}_{c}"] = V.op(
                (lambda cc: lambda v: v.tensor_tensor(
                    V2[:, cc * D:(cc + 1) * D], V2[:, cc * D:(cc + 1) * D], GAM[:], OP.mult))(c),
                reads=(f"V2_{c}_0", f"V2_{c}_1", "GAM"), writes=(f"V2_{c}_0", f"V2_{c}_1"), inc=1)
            ev[f"xB{i}_{c}"] = V.op(
                (lambda pp, cc: lambda v: v.tensor_tensor(
                    XT[pp][:, cc * D:(cc + 1) * D], XT[pp][:, cc * D:(cc + 1) * D],
                    BET[:], OP.add))(p, c),
                reads=("BET", f"XT{p}_{c}"), writes=(f"XT{p}_{c}",), inc=1)
            ev[f"r1{i}_{c}"] = V.op(
                (lambda pp, cc: lambda v: v.scalar_tensor_tensor(
                    V2[:, cc * D:(cc + 1) * D], V2[:, cc * D:(cc + 1) * D], 1.0,
                    XT[pp][:, cc * D:(cc + 1) * D], OP.mult, OP.add,
                    accum_out=st(52 + cc)))(p, c),
                reads=(f"V2_{c}_0", f"V2_{c}_1", f"XT{p}_{c}"),
                writes=(f"V2_{c}_0", f"V2_{c}_1", "P52"), inc=1)
        ev[f"xtfreeV{i}"] = ev[f"r1{i}_{NCH - 1}"]
        emit_xdma(i + 2)
        for c in range(NCH):
            ev[f"rmx{i}_{c}"] = V.op(
                (lambda cc: lambda v: v.reduce_max(
                    st(24 + cc), V2[:, cc * D:(cc + 1) * D], axis=AX.X))(c),
                reads=(f"V2_{c}_0", f"V2_{c}_1"), writes=("P24",),
                waits=[("V", ev[f"r1{i}_{c}"])], inc=1)
            ev[f"rmn{i}_{c}"] = V.op(
                (lambda cc: lambda v: v.tensor_reduce(
                    st(26 + cc), V2[:, cc * D:(cc + 1) * D], AX.X, OP.min))(c),
                reads=(f"V2_{c}_0", f"V2_{c}_1"), writes=("P26",), inc=1)
        V.op(lambda v: v.tensor_scalar(p2(20), p2(52), 1.0 / D, None, OP.mult),
             reads=("P52",), writes=("P20",))
        V.op(lambda v: v.tensor_tensor(p2(38), p2(24), p2(20), OP.subtract),
             reads=("P24", "P20"), writes=("P38",))
        V.op(lambda v: v.tensor_tensor(p2(40), p2(20), p2(26), OP.subtract),
             reads=("P20", "P26"), writes=("P40",))
        V.op(lambda v: v.tensor_tensor(p2(38), p2(38), p2(40), OP.max),
             reads=("P38", "P40"), writes=("P38",))
        V.op(lambda v: v.tensor_scalar(p2(28), p2(38), 1e-12, None, OP.max),
             reads=("P38",), writes=("P28",))
        V.op(lambda v: v.reciprocal(p2(32), p2(28)),
             reads=("P28",), writes=("P32",))
        V.op(lambda v: v.tensor_scalar(p2(34), p2(32), 127.0, None, OP.mult),
             reads=("P32",), writes=("P34",))
        V.op(lambda v: v.tensor_tensor(p2(36), p2(20), p2(34), OP.mult),
             reads=("P20", "P34"), writes=("P36",))
        ev[f"AB{i}"] = V.op(
            lambda v: v.tensor_scalar(p2(36), p2(36), -1.0, None, OP.mult),
            reads=("P36",), writes=("P36",), inc=1)
        lnq = []
        for c in range(NCH):
            lnq.append((1200, (lambda cc: lambda: ev.__setitem__(f"sq3{i}_{cc}", V.op(
                (lambda: lambda v: v.scalar_tensor_tensor(
                    XND[:, D:2 * D], V2[:, cc * D:(cc + 1) * D], 1.0,
                    V2[:, cc * D:(cc + 1) * D], OP.mult, OP.mult,
                    accum_out=st(56 + cc)))(),
                reads=(f"V2_{cc}_0", f"V2_{cc}_1", "XNDMP"),
                writes=("XNDMP", "P56"),
                waits=([("S", ev[f"sxx{i + 2}_{NCH - 1}"])]
                       if (cc == 0 and f"sxx{i + 2}_{NCH - 1}" in ev) else []),
                inc=1)))(c)))
        lnq.append((220, lambda: V.op(
            lambda v: v.tensor_scalar(p2(56), p2(56), 1.0 / D, None, OP.mult),
            reads=("P56",), writes=("P56",))))
        lnq.append((220, lambda: V.op(
            lambda v: v.tensor_tensor(p2(62), p2(20), p2(20), OP.mult),
            reads=("P20",), writes=("P62",))))
        lnq.append((220, lambda: V.op(
            lambda v: v.tensor_tensor(p2(56), p2(56), p2(62), OP.subtract),
            reads=("P56", "P62"), writes=("P56",))))
        lnq.append((220, lambda: V.op(
            lambda v: v.tensor_scalar(p2(56), p2(56), st(44), None, OP.add),
            reads=("P56", "ST44"), writes=("P56",))))
        lnq += newton_rsqrt_thunks(56, 22)
        lnq.append((220, lambda: V.op(
            lambda v: v.tensor_tensor(p2(30), p2(28), p2(22), OP.mult),
            reads=("P28", "P22"), writes=("P30",))))
        lnq.append((220, lambda: V.op(
            lambda v: v.tensor_scalar(p2(30), p2(30), -1.0 / 127.0, None, OP.mult),
            reads=("P30",), writes=("P30", "ST300", "ST301"))))
        for c in range(NCH):
            lnq.append((1250, (lambda cc: lambda: ev.__setitem__(f"rB{i}_{cc}", V.op(
                (lambda: lambda v: v.tensor_tensor(
                    V2[:, cc * D:(cc + 1) * D], V2[:, cc * D:(cc + 1) * D],
                    FBI[:], OP.add))(),
                reads=(f"V2_{cc}_0", f"V2_{cc}_1", "FBI"),
                writes=(f"V2_{cc}_0", f"V2_{cc}_1"),
                waits=[("V", ev[f"tq{i}_{NCH - 1}"])] if cc == 0 else [],
                inc=1)))(c)))
        lnq_def[i] = lnq
        qth = []
        for c in range(NCH):
            qth.append((900, (lambda cc: lambda: ev.__setitem__(f"tq{i}_{cc}", V.op(
                (lambda: lambda v: v.tensor_scalar(
                    TQS[:], V2[:, cc * D:(cc + 1) * D],
                    st(34 + cc), st(36 + cc), OP.mult, OP.add))(),
                reads=("P34", "P36", "TQS", f"V2_{cc}_0", f"V2_{cc}_1"),
                writes=("TQS",),
                waits=([("V", ev[f"epi{i - 1}_1"] if f"epi{i - 1}_1" in ev
                        else ev["prepV"])] if cc == 0 else []),
                inc=1)))(c)))
            qth.append((750, (lambda cc: lambda: ev.__setitem__(f"qk{i}_{cc}", V.op(
                (lambda: lambda v: v.tensor_copy(I8Q[:, 0:D], TQS[:]))(),
                reads=("TQS", "I8Q"), writes=("I8Q",),
                waits=([("T", ev[f"fm1T{i + 1}_{NCH - 1}"])]
                       if (cc == 0 and f"fm1T{i + 1}_{NCH - 1}" in ev) else []),
                inc=1)))(c)))
            qth.append((750, (lambda cc: lambda: ev.__setitem__(f"qk2{i}_{cc}", V.op(
                (lambda: lambda v: v.tensor_copy(
                    XQB[:, cc * D:(cc + 1) * D], I8Q[:, 0:D]))(),
                reads=("I8Q",), writes=(f"XQB{cc}",), inc=1)))(c)))
            qth.append((1300, (lambda cc: lambda: ev.__setitem__(f"qr0{i}_{cc}", V.op(
                (lambda: lambda v: v.tensor_tensor(
                    TQS[:], TQS[:], XQB[:, cc * D:(cc + 1) * D], OP.subtract))(),
                reads=("TQS", f"XQB{cc}"), writes=("TQS",), inc=1)))(c)))
            qth.append((1300, (lambda cc: lambda: V.op(
                (lambda: lambda v: v.scalar_tensor_tensor(
                    XQB[:, cc * D:(cc + 1) * D], TQS[:], 0.5,
                    XQB[:, cc * D:(cc + 1) * D], OP.is_ge, OP.add))(),
                reads=("TQS", f"XQB{cc}"), writes=(f"XQB{cc}",)))(c)))
            qth.append((1300, (lambda cc: lambda: ev.__setitem__(f"qr{i}_{cc}", V.op(
                (lambda: lambda v: v.scalar_tensor_tensor(
                    XQB[:, cc * D:(cc + 1) * D], TQS[:], -0.5,
                    XQB[:, cc * D:(cc + 1) * D], OP.is_le, OP.subtract))(),
                reads=("TQS", f"XQB{cc}"), writes=(f"XQB{cc}",), inc=1)))(c)))
        qth_def[i] = qth

    def ffn_block(i, last=False):
        FM2A = FM2[:].rearrange("p (k t) -> p k t", k=KD)
        for c in range(NCH):
            vb = vb4 if c == 0 else vb5
            for k in range(KD):
                w_ = []
                if k == 0:
                    w_.append(("V", ev[f"qr{i}_{NCH - 1}"]))
                    w_.append(("S", ev[f"fm1ev{i + 1}_{c}"] if f"fm1ev{i + 1}_{c}" in ev else ev[f"fm1ev{i}_{c}"]))
                    if f"zt{i + 1}_{MI - 1}" in ev:
                        w_.append(("V", ev[f"zt{i + 1}_{MI - 1}"]))
                ev[f"fm2T{i}_{c}"] = T.op(
                    (lambda cc, kk, vv: lambda t: t.transpose(
                        vv[:, kk * 128:(kk + 1) * 128],
                        XQB[:, cc * D + kk * 128:cc * D + (kk + 1) * 128], IDT[:]))(c, k, vb),
                    waits=w_, inc=1 if k == KD - 1 else 0)
            ev[f"fm2ev{i}_{c}"] = S.op(
                (lambda cc, vv: lambda s: s.copy(
                    FM2A[:, 0:KD, cc * 128:(cc + 1) * 128], vv[:, 0:1024]))(c, vb),
                reads=(), writes=(f"FM2_{c}",),
                waits=[("T", ev[f"fm2T{i}_{c}"])], inc=1)
        for c in range(NCH):
            for h in range(2):
                bank = pb[h]
                w_ = []
                if c == 0 and h == 0:
                    w_.append(("S", ev[f"fm2ev{i}_{NCH - 1}"]))
                    if f"gev{i + 1}_{MI - 1}" in ev:
                        w_.append(("S", ev[f"gev{i + 1}_{MI - 1}"]))
                    elif f"o1{i - 1}_1_1" in ev:
                        w_.append(("V", ev[f"o1{i - 1}_1_1"]))
                if c == 1 and h == 0:
                    w_.append(("V", ev[f"o1{i}_0_1"]))
                for k in range(KD):
                    ev[f"ffn{i}_{c}_{h}"] = T.op(
                        (lambda cc, hh, kk, bk: lambda t: t.matmul(
                            bk[:, 0:512],
                            FM2[:, kk * TT + cc * 128:kk * TT + (cc + 1) * 128],
                            Wq[:, kk * D + hh * 512:kk * D + (hh + 1) * 512],
                            start=(kk == 0), stop=(kk == KD - 1)))(c, h, k, bank),
                        waits=w_ if k == 0 else [], inc=1 if k == KD - 1 else 0)
            for h in range(2):
                bank = pb[h]
                wo1 = [("T", ev[f"ffn{i}_{c}_{h}"]), ("V", ev[f"rB{i}_{c}"])]
                if c == 0 and h == 0:
                    wo1.append(("V", ev[f"zlast{i + 1}"] if f"zlast{i + 1}" in ev
                                else ev[f"zlast{i}"]))
                ev[f"o1{i}_{c}_{h}"] = V.op(
                    (lambda cc, hh, bk: lambda v: v.scalar_tensor_tensor(
                        SCR[:, hh * 512:(hh + 1) * 512], bk[:, 0:512], st(30 + cc),
                        WSC[:, hh * 512:(hh + 1) * 512], OP.mult, OP.mult))(c, h, bank),
                    reads=(f"ST30{c}", "WSC"), writes=(f"SCR{h}",),
                    waits=wo1, inc=1)
            ev[f"epi{i}_{c}"] = V.op(
                (lambda cc: lambda v: v.tensor_tensor(
                    V2[:, cc * D:(cc + 1) * D], SCR[:],
                    V2[:, cc * D:(cc + 1) * D], OP.add))(c),
                reads=("SCR0", "SCR1", f"V2_{c}_0", f"V2_{c}_1"),
                writes=(f"V2_{c}_0", f"V2_{c}_1"), inc=1)
            r0 = (i - NT_WARM) * TT + c * 128
            if c == 0:
                ev[f"odma{i}"] = Gq.op(
                    (lambda cc, rr: lambda g: g.dma_start(
                        out=OUTd[rr:rr + 128, :], in_=V2[:, cc * D:(cc + 1) * D]))(c, r0),
                    waits=[("V", ev[f"epi{i}_{c}"])], inc=16)
            else:
                ev[f"odmaq{i}"] = Q.op(
                    (lambda cc, rr: lambda q: q.dma_start(
                        out=OUTd[rr:rr + 128, :], in_=V2[:, cc * D:(cc + 1) * D]))(c, r0),
                    waits=[("V", ev[f"epi{i}_{c}"])], inc=16)

    NTILES_ALL = NT_WARM + NT_MAIN
    mixer_stats(0)
    for i in range(NTILES_ALL):
        if i == NT_WARM:
            prep_late()
        if i - 1 >= NT_WARM:
            ln_block(i - 1)
        mixer_main(i)
        if i >= NT_WARM:
            gate_block(i)
        if i - 1 >= NT_WARM:
            ffn_block(i - 1)
        if i >= NT_WARM and i + 1 < NTILES_ALL:
            stats_S_post(i + 1)
        if i >= NT_WARM:
            v2_block(i)
    ln_block(NTILES_ALL - 1)
    for _, t in qth_def.pop(NTILES_ALL - 1):
        t()
    for _, t in lnq_def.pop(NTILES_ALL - 1):
        t()
    ffn_block(NTILES_ALL - 1, last=True)

    total_odma = 16 * NT_MAIN
    total_q = Q.cnt
    if debug:
        dbg = {}
        for nm, buf in (("DBG_FM1", FM1), ("DBG_FM2", FM2), ("DBG_BU", BU),
                        ("DBG_HSB", HSB), ("DBG_XQB", XQB), ("DBG_XB", XT[(NTILES_ALL - 1) % 2]),
                        ("DBG_ST", ST), ("DBG_HC", HC), ("DBG_V2", V2),
                        ("DBG_WQ", Wq), ("DBG_WSC", WSC), ("DBG_GAM", GAM),
                        ("DBG_FBI", FBI), ("DBG_BET", BET)):
            shp = [buf.shape[0], buf.shape[1]] if hasattr(buf, 'shape') else None
            d = nc.dram_tensor(nm, list(buf.shape), buf.dtype, kind="ExternalOutput").ap()
            dbg[nm] = (d, buf)
        nd = 0
        for nm, (d, buf) in dbg.items():
            Gq.op((lambda dd, bb: lambda g: g.dma_start(out=dd, in_=bb[:]))(d, buf),
                  waits=[("G", total_odma)] if nd == 0 else [], inc=16)
            nd += 1
        total_odma += 16 * nd
    Gq.op(lambda g: g.nop(), waits=[("G", total_odma), ("Q", total_q)])

    with es:
        with nc.Block() as block:
            @block.sync
            def _(sync):
                for it in Q.items:
                    it(sync, sems)

            @block.tensor
            def _(t):
                for it in T.items:
                    it(t, sems)

            @block.vector
            def _(v):
                for it in V.items:
                    it(v, sems)

            @block.scalar
            def _(s):
                for it in S.items:
                    it(s, sems)

            @block.gpsimd
            def _(g):
                for it in Gq.items:
                    it(g, sems)

    return nc


def kernel(L0_attn_x, L0_attn_W_in, L0_attn_conv_w, L0_attn_conv_b,
           L0_attn_W_B, L0_attn_W_C, L0_attn_D_param, L0_attn_W_out,
           L0_attn_ln_gamma, L0_attn_ln_beta, L0_ffn_weight, L0_ffn_bias):
    x = np.ascontiguousarray(np.asarray(L0_attn_x, np.float32))
    B, Tn, Dn = x.shape

    conv_w = np.asarray(L0_attn_conv_w, np.float32).reshape(INNER, 4)
    CWh = np.ascontiguousarray(conv_w.reshape(MI, 128, 4).transpose(1, 0, 2).reshape(128, MI * 4))
    CBh = np.ascontiguousarray(np.asarray(L0_attn_conv_b, np.float32).reshape(MI, 128).T)
    DPh = np.ascontiguousarray(np.asarray(L0_attn_D_param, np.float32).reshape(MI, 128).T)
    GAMh = np.ascontiguousarray(np.asarray(L0_attn_ln_gamma, np.float32).reshape(1, D))
    BETh = np.ascontiguousarray(np.asarray(L0_attn_ln_beta, np.float32).reshape(1, D))
    FBh = np.ascontiguousarray(np.asarray(L0_ffn_bias, np.float32).reshape(1, D))
    IDTh = np.eye(128, dtype=ml_dtypes.bfloat16)
    WINh = np.ascontiguousarray(np.asarray(L0_attn_W_in, np.float32))
    WOUTh = np.ascontiguousarray(np.asarray(L0_attn_W_out, np.float32))
    WBh = np.ascontiguousarray(np.asarray(L0_attn_W_B, np.float32))
    WCh = np.ascontiguousarray(np.asarray(L0_attn_W_C, np.float32))
    FWh = np.ascontiguousarray(np.asarray(L0_ffn_weight, np.float32))

    in_maps = []
    for c in range(8):
        b, h = c // 2, c % 2
        main = x[b, h * NTOK:(h + 1) * NTOK]
        if h == 0:
            warm = np.zeros((WARM, D), np.float32)
        else:
            warm = x[b, NTOK - WARM:NTOK]
        X = np.ascontiguousarray(np.concatenate([warm, main], axis=0))
        FLGh = np.full((STATE, 1), float(h), np.float32)
        in_maps.append({
            "X": X, "WIN": WINh, "WOUT": WOUTh, "WB": WBh, "WC": WCh,
            "FW": FWh, "CW": CWh, "CB": CBh, "DP": DPh, "GAM": GAMh,
            "BET": BETh, "FB": FBh, "IDT": IDTh, "FLG": FLGh,
        })

    nc = build_nc()
    res = run_bass_kernel_spmd(nc, in_maps, core_ids=list(range(8)))
    out = np.empty((B, Tn, Dn), np.float32)
    for c in range(8):
        b, h = c // 2, c % 2
        out[b, h * NTOK:(h + 1) * NTOK] = res.results[c]["OUT"]
    return out


# version 26 (final)
# speedup vs baseline: 1.0283x; 1.0283x over previous
import numpy as np
import ml_dtypes
import concourse.bass as bass
import concourse.mybir as mybir
from concourse.bass_utils import run_bass_kernel_spmd
from concourse.alu_op_type import AluOpType

F32 = mybir.dt.float32
BF16 = mybir.dt.bfloat16
I32 = mybir.dt.int32
I8 = mybir.dt.int8
AFT = mybir.ActivationFunctionType
AX = mybir.AxisListType
OP = AluOpType

D = 1024
INNER = 2048
STATE = 64
NTOK = 4096
WARM = 768
TT = 256
NCH = TT // 128
NT_MAIN = NTOK // TT
NT_WARM = WARM // TT
KD = D // 128
MI = INNER // 128
LN_EPS = 1e-5
HALO = 3
XSLOT = TT + 8


class Stream:

    def __init__(self, key, track_hazards, dma_chain=False):
        self.key = key
        self.items = []
        self.cnt = 0
        self.track = track_hazards
        self.dma_chain = dma_chain
        self.wkey = {}
        self.rkey = {}
        self.guard = 0

    def op(self, fn, reads=(), writes=(), waits=(), inc=0):
        key = self.key
        waits = list(waits)
        if self.dma_chain and self.cnt > 0:
            waits.append((key, self.cnt))
        if self.track:
            hz = 0
            for k in set(reads) | set(writes):
                hz = max(hz, self.wkey.get(k, 0))
            for k in set(writes):
                hz = max(hz, self.rkey.get(k, 0))
            selfwait = hz if hz > self.guard else 0
            if selfwait:
                waits.append((key, selfwait))
                self.guard = selfwait
            inc = 1
            newcnt = self.cnt + 1
            for k in writes:
                self.wkey[k] = newcnt
            for k in reads:
                self.rkey[k] = newcnt
        else:
            selfwait = 0

        def run(eng, sems):
            for sk, val in waits:
                eng.wait_ge(sems[sk], val)
            ins = fn(eng)
            if inc:
                ins.then_inc(sems[key], inc)
        self.items.append(run)
        if inc:
            self.cnt += inc
        return self.cnt


def build_nc(debug=False):
    nc = bass.Bass("TRN2", target_bir_lowering=False)

    Xd = nc.dram_tensor("X", [WARM + NTOK, D], F32, kind="ExternalInput").ap()
    OUTd = nc.dram_tensor("OUT", [NTOK, D], F32, kind="ExternalOutput").ap()
    WINd = nc.dram_tensor("WIN", [2 * INNER, D], F32, kind="ExternalInput").ap()
    WOUTd = nc.dram_tensor("WOUT", [D, INNER], F32, kind="ExternalInput").ap()
    WBd = nc.dram_tensor("WB", [STATE, INNER], F32, kind="ExternalInput").ap()
    WCd = nc.dram_tensor("WC", [INNER, STATE], F32, kind="ExternalInput").ap()
    FWd = nc.dram_tensor("FW", [D, D], F32, kind="ExternalInput").ap()
    CWd = nc.dram_tensor("CW", [128, MI * 4], F32, kind="ExternalInput").ap()
    CBd = nc.dram_tensor("CB", [128, MI], F32, kind="ExternalInput").ap()
    DPd = nc.dram_tensor("DP", [128, MI], F32, kind="ExternalInput").ap()
    GAMd = nc.dram_tensor("GAM", [1, D], F32, kind="ExternalInput").ap()
    BETd = nc.dram_tensor("BET", [1, D], F32, kind="ExternalInput").ap()
    FBd = nc.dram_tensor("FB", [1, D], F32, kind="ExternalInput").ap()
    IDTd = nc.dram_tensor("IDT", [128, 128], BF16, kind="ExternalInput").ap()
    FLGd = nc.dram_tensor("FLG", [STATE, 1], F32, kind="ExternalInput").ap()
    WSH = nc.dram_tensor("WSH", [1, D], F32, kind="Internal").ap()

    from contextlib import ExitStack
    es = ExitStack()

    def sb(name, shape, dt):
        return es.enter_context(nc.sbuf_tensor(name, shape, dt))

    Wi = sb("Wi", [128, 2 * MI * KD * 128], BF16)
    Wo = sb("Wo", [128, MI * D], BF16)
    Wq = sb("Wq", [128, KD * D], BF16)
    Wb = sb("Wb", [128, MI * STATE], BF16)
    Wc = sb("Wc", [128, INNER], BF16)
    Dcv = sb("Dcv", [128, MI * 4 * 128], BF16)
    GAM = sb("GAMb", [128, D], F32)
    BET = sb("BETb", [128, D], F32)
    WSC = sb("WSCb", [128, D], F32)
    FBI = sb("FBIb", [128, D], F32)
    CB = sb("CBb", [128, MI], F32)
    DP = sb("DPb", [128, MI], F32)
    CW = sb("CWb", [128, MI * 4], F32)
    IDT = sb("IDTb", [128, 128], BF16)
    DEC = sb("DECb", [STATE, TT], F32)
    FLG = sb("FLGb", [STATE, 1], F32)
    ONE1 = sb("ONE1", [1, 128], F32)

    def alias(name, shape, dtype, backing, byte_off=0):
        h = nc.virtual_sbuf_tensor(name, shape, dtype)
        nc._tensor(h.name, list(h.shape), dtype, type="SB")
        mloc = nc.lookup_mloc(h)
        mloc.allocated = True
        mloc.addr = nc.lookup_mloc(backing).addr + byte_off
        return h

    XT = [sb("XT0", [128, NCH * D], F32), sb("XT1", [128, NCH * D], F32)]
    XN1B = sb("XN1B", [128, NCH * D], BF16)
    FM1 = sb("FM1", [128, KD * TT], BF16)
    FM2 = sb("FM2", [128, KD * TT], BF16)
    XIN = sb("XIN", [128, 2 * XSLOT], BF16)
    HLO = sb("HLO", [128, MI * 4], BF16)
    U = sb("U", [128, MI * TT], BF16)
    G = sb("G", [128, 8 * TT], BF16)
    Z = sb("Z", [128, 2 * TT], BF16)
    SCR = alias("SCRv", [128, D], F32, G)
    BU = sb("BUb", [STATE, TT], F32)
    HSB = sb("HSBb", [STATE, TT], BF16)
    HC = sb("HCb", [STATE, 1], F32)
    V2 = sb("V2b", [128, NCH * D], F32)
    XQB = sb("XQB", [128, NCH * D], BF16)
    ZT = alias("ZTv", [128, TT], F32, XN1B)
    ST = CW

    I32B = alias("I32B", [128, 1024], I32, FM2)
    I32U = alias("I32U", [128, 1024], I32, U)
    SQD = alias("SQDv", [128, 1024], F32, U, byte_off=MI * TT)
    XND = XN1B
    TQS = alias("TQSv", [128, 1024], F32, FM2)
    I8Q = alias("I8Qv", [128, 1024], I8, XN1B, byte_off=1024)
    UKEYS = tuple(f"U{j}" for j in range(16))
    def usq(c):
        return XN1B[:, D:2 * D]

    def st(col):
        return ST[:, col:col + 1]

    pb = {}
    for j in (0, 1, 2, 3, 6, 7):
        pb[j] = es.enter_context(nc.psum_tensor(f"pb{j}", [128, 512], F32))
    vb4 = es.enter_context(nc.psum_tensor("vb4", [128, 1024], BF16))
    vb5 = es.enter_context(nc.psum_tensor("vb5", [128, 1024], BF16))

    def psum_alias(name, shape, dtype, backing):
        h = nc.virtual_psum_tensor(name, shape, dtype)
        nc._tensor(h.name, list(h.shape), dtype, type="PSUM")
        mloc = nc.lookup_mloc(h)
        bm = nc.lookup_mloc(backing)
        mloc.allocated = True
        mloc.addr = bm.addr
        mloc.bank = bm.bank
        mloc.base = bm.base
        return h

    YB = [psum_alias("yb4f", [128, 512], F32, vb4),
          psum_alias("yb5f", [128, 512], F32, vb5)]
    FB = [psum_alias("fb4f", [128, 512], F32, vb4),
          psum_alias("fb5f", [128, 512], F32, vb5)]

    sems = {}
    for k in ("T", "V", "S", "Q", "G", "R"):
        sems[k] = es.enter_context(nc.semaphore("sem" + k))

    T = Stream("T", False)
    V = Stream("V", True)
    S = Stream("S", True)
    Q = Stream("Q", False, dma_chain=True)
    Gq = Stream("G", False, dma_chain=True)
    ev = {}
    lnq_def = {}
    qth_def = {}

    ev["cdma"] = Q.op(lambda q: q.dma_start(out=CW[:], in_=CWd), inc=16)
    ev["cbdma"] = Q.op(lambda q: q.dma_start(out=CB[:], in_=CBd), inc=16)
    ev["dpdma"] = Q.op(lambda q: q.dma_start(out=DP[:], in_=DPd), inc=16)
    ev["idma"] = Q.op(lambda q: q.dma_start(out=IDT[:], in_=IDTd), inc=16)
    ev["fdma"] = Q.op(lambda q: q.dma_start(out=FLG[:], in_=FLGd), inc=16)
    V.op(lambda v: v.memset(DEC[:], 0.95), writes=("DEC",))
    V.op(lambda v: v.memset(ONE1[:], 1.0), writes=("ONE1",))
    V.op(lambda v: v.memset(HC[:], 0.0), writes=("HC",))
    ev["hlomem"] = V.op(lambda v: v.memset(HLO[:], 0.0), writes=("HLO",), inc=1)
    for f in range(MI):
        for kk in range(4):
            lastd = (f == MI - 1 and kk == 3)
            ev["diag"] = V.op(
                (lambda ff, k2: lambda v: v.tensor_scalar(
                    Dcv[:, (ff * 4 + k2) * 128:(ff * 4 + k2 + 1) * 128],
                    IDT[:], CW[:, ff * 4 + k2:ff * 4 + k2 + 1], None, OP.mult))(f, kk),
                reads=("IDT", "CW"), writes=("Dcv",),
                waits=[("Q", ev["idma"])] if (f == 0 and kk == 0) else [],
                inc=1 if lastd else 0)
    V.op(lambda v: v.memset(st(44), LN_EPS), reads=("CW",), writes=("CW", "ST44"))

    def prep_win_tile(m, prev_cast, prev_tp, prev_ev):
        par = m % 2
        stage = V2[:, par * D:(par + 1) * D]
        bstage_lo = par * D
        BST = U if m < MI else XQB
        bkeys = ("U",) if m < MI else ("XQB0", "XQB1")
        ev[f"wdma{m}"] = (Q if m % 2 == 0 else Gq).op(
            (lambda ss, mm: lambda q: q.dma_start(out=ss, in_=WINd[mm * 128:(mm + 1) * 128, :]))(stage, m),
            waits=[("S", prev_cast)] if prev_cast else [], inc=16)
        ev[f"wcast{m}"] = S.op(
            (lambda ss, lo, bb: lambda s: s.activation(bb[:, lo:lo + D], ss, AFT.Copy))(stage, bstage_lo, BST),
            reads=bkeys, writes=bkeys,
            waits=[("Q" if m % 2 == 0 else "G", ev[f"wdma{m}"])] + ([("T", prev_tp)] if prev_tp else []),
            inc=1)
        vb = vb4 if par == 0 else vb5
        for k in range(KD):
            w_ = []
            if k == 0:
                w_.append(("S", ev[f"wcast{m}"]))
                if prev_ev:
                    w_.append(("S", prev_ev))
            ev[f"wtp{m}"] = T.op(
                (lambda kk, vv, lo, bb: lambda t: t.transpose(
                    vv[:, kk * 128:(kk + 1) * 128], bb[:, lo + kk * 128:lo + (kk + 1) * 128], IDT[:]))(k, vb, bstage_lo, BST),
                waits=w_, inc=1 if k == KD - 1 else 0)
        ev[f"wev{m}"] = S.op(
            (lambda vv, mm: lambda s: s.copy(
                Wi[:, mm * KD * 128:(mm + 1) * KD * 128], vv[:, 0:1024]))(vb, m),
            reads=(), writes=(),
            waits=[("T", ev[f"wtp{m}"])], inc=1)

    pc = pt = pe = None
    for m in range(MI):
        prep_win_tile(m,
                      ev.get(f"wcast{m - 2}"),
                      ev.get(f"wtp{m - 2}"),
                      ev.get(f"wev{m - 2}"))

    ev["wbdma"] = Q.op(lambda q: q.dma_start(out=V2[0:STATE, 0:INNER], in_=WBd),
                       waits=[("S", ev[f"wcast{MI - 1}"])], inc=16)
    ev["wbcast"] = S.op(lambda s: s.activation(XQB[0:STATE, 0:INNER], V2[0:STATE, 0:INNER], AFT.Copy),
                        reads=("XQB0", "XQB1"), writes=("XQB0", "XQB1"),
                        waits=[("Q", ev["wbdma"]), ("T", ev[f"wtp{MI - 1}"])], inc=1)
    for f in range(MI):
        w_ = [("S", ev["wbcast"])] if f == 0 else []
        if f == 0:
            w_.append(("S", ev[f"wev{MI - 1}"]))
        ev["wbtp"] = T.op(
            (lambda ff: lambda t: t.transpose(
                vb4[:, ff * STATE:(ff + 1) * STATE],
                XQB[0:STATE, ff * 128:(ff + 1) * 128], IDT[0:STATE, 0:STATE]))(f),
            waits=w_, inc=1 if f == MI - 1 else 0)
    ev["wbev"] = S.op(lambda s: s.copy(Wb[:], vb4[:, 0:MI * STATE]),
                      waits=[("T", ev["wbtp"])], inc=1)

    def prep_late():
        ev["wcdma"] = Q.op(
            lambda q: q.dma_start(
                out=V2[:, 0:MI * STATE].rearrange("p (f s) -> p f s", f=MI),
                in_=WCd.rearrange("(f p) s -> p f s", p=128)),
            waits=[("S", ev["wbcast"])], inc=16)
        ev["wccast"] = S.op(lambda s: s.activation(XQB[:, 0:MI * STATE], V2[:, 0:MI * STATE], AFT.Copy),
                            reads=("XQB0", "XQB1"), writes=("XQB0", "XQB1"),
                            waits=[("Q", ev["wcdma"]), ("T", ev["wbtp"])], inc=1)
        for half in range(2):
            vb = vb4 if half == 0 else vb5
            for j in range(8):
                f = half * 8 + j
                w_ = []
                if f == 0:
                    w_ += [("S", ev["wccast"]), ("S", ev["wbev"])]
                    if f"fm1ev{NT_WARM - 1}_{NCH - 1}" in ev:
                        w_.append(("S", ev[f"fm1ev{NT_WARM - 1}_{NCH - 1}"]))
                if f == 8:
                    w_ += [("S", ev["wcev0"])]
                ev["wctp"] = T.op(
                    (lambda ff, jj, vv: lambda t: t.transpose(
                        vv[0:STATE, jj * 128:(jj + 1) * 128],
                        XQB[:, ff * STATE:(ff + 1) * STATE], IDT[:]))(f, j, vb),
                    waits=w_, inc=1 if j == 7 else 0)
            ev[f"wcev{half}"] = S.op(
                (lambda vv, hh: lambda s: s.copy(
                    Wc[0:STATE, hh * 1024:(hh + 1) * 1024], vv[0:STATE, 0:1024]))(vb, half),
                waits=[("T", ev["wctp"])], inc=1)

        for m in range(MI, 2 * MI):
            prep_win_tile(m,
                          ev.get(f"wcast{m - 2}") if m >= MI + 2 else ev["wccast"],
                          ev.get(f"wtp{m - 2}") if m >= MI + 2 else ev["wctp"],
                          ev.get(f"wev{m - 2}") if m >= MI + 2 else ev["wcev1"])

        WoA = Wo[:].rearrange("p (k d) -> p k d", k=MI)
        for d8 in range(8):
            ev[f"wodma{d8}"] = (Q if d8 % 2 == 0 else Gq).op(
                (lambda dd: lambda q: q.dma_start(out=V2[:, 0:INNER],
                                                  in_=WOUTd[dd * 128:(dd + 1) * 128, :]))(d8),
                waits=[("S", ev[f"wocast{d8 - 1}"] if d8 >= 1 else ev[f"wcast{2 * MI - 1}"])],
                inc=16)
            ev[f"wocast{d8}"] = S.op(
                lambda s: s.activation(XQB[:, 0:INNER], V2[:, 0:INNER], AFT.Copy),
                reads=("XQB0", "XQB1"), writes=("XQB0", "XQB1"),
                waits=[("Q" if d8 % 2 == 0 else "G", ev[f"wodma{d8}"]),
                       ("T", ev[f"wotp{d8 - 1}"] if d8 >= 1 else ev[f"wtp{2 * MI - 1}"])],
                inc=1)
            for k in range(MI):
                vb = vb4 if k < 8 else vb5
                w_ = []
                if k == 0:
                    w_.append(("S", ev[f"wocast{d8}"]))
                    w_.append(("S", ev[f"woev{d8 - 1}"] if d8 >= 1 else ev[f"wev{2 * MI - 1}"]))
                ev[f"wotp{d8}"] = T.op(
                    (lambda kk, vv: lambda t: t.transpose(
                        vv[:, (kk % 8) * 128:((kk % 8) + 1) * 128],
                        XQB[:, kk * 128:(kk + 1) * 128], IDT[:]))(k, vb),
                    waits=w_, inc=1 if k == MI - 1 else 0)

            def woev(dd):
                def go(s):
                    s.copy(WoA[:, 0:8, dd * 128:(dd + 1) * 128], vb4[:, 0:1024])
                    return s.copy(WoA[:, 8:16, dd * 128:(dd + 1) * 128], vb5[:, 0:1024])
                return go
            ev[f"woev{d8}"] = S.op(woev(d8), waits=[("T", ev[f"wotp{d8}"])], inc=1)

        WqA = Wq[:].rearrange("p (k d) -> p k d", k=KD)
        for t8 in range(8):
            par = t8 % 2
            vstg = V2[:, par * D:(par + 1) * D]
            TB = SCR if par == 0 else TQS
            KB = I32B if par == 0 else I32U
            QB = XQB if par == 0 else FM1
            VBT = vb4 if par == 0 else vb5
            cR, cW, cN = (50, 51, 45) if par == 0 else (46, 47, 48)
            tkey = ("SCR0", "SCR1") if par == 0 else ("TQS",)
            kkey = ("FM2i",) if par == 0 else ("I32U",) + UKEYS
            qkey = ("XQB0", "XQB1") if par == 0 else ("FM1_0", "FM1_1")
            qk = ("Q", "G")[par]
            ev[f"fqdma{t8}"] = (Q if par == 0 else Gq).op(
                (lambda tt, ss: lambda q: q.dma_start(out=ss,
                                                      in_=FWd[tt * 128:(tt + 1) * 128, :]))(t8, vstg),
                waits=[("S", ev[f"fqt{t8 - 2}"] if t8 >= 2 else ev["wocast7"])] +
                      ([("V", ev[f"fqred{t8 - 2}"])] if t8 >= 2 else []),
                inc=16)
            ev[f"fqred{t8}"] = V.op(
                (lambda ss, cc: lambda v: v.reduce_max(
                    st(cc), ss, axis=AX.X, apply_absolute_value=True))(vstg, cR),
                reads=(), writes=(f"ST{cR}",),
                waits=[(qk, ev[f"fqdma{t8}"])], inc=1)
            V.op((lambda a, b: lambda v: v.tensor_scalar(st(b), st(a), 1e-8, 1.0 / 127.0, OP.max, OP.mult))(cR, cW),
                 reads=(f"ST{cR}",), writes=(f"ST{cW}",),
                 waits=[("Q", ev[f"fqws{t8 - 2}"])] if t8 >= 2 else [])
            V.op((lambda a, b: lambda v: v.reciprocal(st(a), st(b)))(cR, cW),
                 reads=(f"ST{cW}",), writes=(f"ST{cR}",))
            ev[f"fqrec{t8}"] = V.op(
                (lambda a, b: lambda v: v.tensor_scalar(st(b), st(a), -1.0, None, OP.mult))(cR, cN),
                reads=(f"ST{cR}",), writes=(f"ST{cN}",), inc=1)
            ev[f"fqt{t8}"] = S.op(
                (lambda tb, ss, cc: lambda s: s.activation(tb[:], ss, AFT.Identity, scale=st(cc)))(TB, vstg, cN),
                reads=tkey, writes=tkey + (("G0", "G1", "G2", "G3", "G4", "G5", "G6", "G7") if par == 0 else ()),
                waits=[("V", ev[f"fqrec{t8}"])] +
                      ([("V", ev[f"fqq{t8 - 2}"])] if t8 >= 2 else
                       [("T", ev["wotp7"])]),
                inc=1)
            ev[f"fqk{t8}"] = S.op(
                (lambda kb, tb: lambda s: s.copy(kb[:, 0:D], tb[:]))(KB, TB),
                reads=tkey + kkey, writes=kkey, inc=1)
            ev[f"fqk2{t8}"] = S.op(
                (lambda qb, kb: lambda s: s.copy(qb[:, 0:D], kb[:, 0:D]))(QB, KB),
                reads=kkey, writes=qkey[:1], inc=1)
            ev[f"fqr{t8}"] = V.op(
                (lambda tb, qb: lambda v: v.tensor_tensor(tb[:], tb[:], qb[:, 0:D], OP.subtract))(TB, QB),
                reads=tkey + qkey[:1], writes=tkey,
                waits=[("S", ev[f"fqk2{t8}"])], inc=1)
            V.op((lambda tb, qb: lambda v: v.scalar_tensor_tensor(
                qb[:, 0:D], tb[:], 0.5, qb[:, 0:D], OP.is_ge, OP.add))(TB, QB),
                 reads=tkey + qkey[:1], writes=qkey[:1])
            ev[f"fqq{t8}"] = V.op(
                (lambda tb, qb: lambda v: v.scalar_tensor_tensor(
                    qb[:, D:2 * D], tb[:], -0.5, qb[:, 0:D], OP.is_le, OP.subtract))(TB, QB),
                reads=tkey + qkey, writes=qkey[1:], inc=1)
            for k in range(KD):
                w_ = []
                if k == 0:
                    w_.append(("V", ev[f"fqq{t8}"]))
                    w_.append(("S", ev[f"fqev{t8 - 2}"] if t8 >= 2 else ev["woev7"]))
                ev[f"fqtp{t8}"] = T.op(
                    (lambda kk, vv, qb: lambda t: t.transpose(
                        vv[:, kk * 128:(kk + 1) * 128],
                        qb[:, D + kk * 128:D + (kk + 1) * 128], IDT[:]))(k, VBT, QB),
                    waits=w_, inc=1 if k == KD - 1 else 0)
            ev[f"fqev{t8}"] = S.op(
                (lambda tt, vv: lambda s: s.copy(WqA[:, 0:KD, tt * 128:(tt + 1) * 128],
                                                 vv[:, 0:1024]))(t8, VBT),
                waits=[("T", ev[f"fqtp{t8}"])], inc=1)
            ev[f"fqws{t8}"] = Q.op(
                (lambda tt, cc: lambda q: q.dma_start(out=WSH[0:1, tt * 128:(tt + 1) * 128],
                                                      in_=st(cc)))(t8, cW),
                waits=[("V", ev[f"fqrec{t8}"])],
                inc=16)


        ev["wsrow"] = Q.op(lambda q: q.dma_start(out=V2[0:1, 0:D], in_=WSH),
                           waits=[("S", ev["fqt7"]), ("Q", ev["fqws7"])], inc=16)
        ev["gamdma"] = Q.op(lambda q: q.dma_start(out=V2[0:1, D:2 * D], in_=GAMd), inc=16)
        ev["negws"] = V.op(
            lambda v: v.tensor_copy(st(45), st(45)),
            reads=("ST45",), writes=("ST45",),
            waits=[("Q", ev["wsrow"])], inc=1)
        bco = [("WSC", WSC, V2[0:1, 0:D], "wsrow"), ("GAM", GAM, V2[0:1, D:2 * D], "gamdma"),
               ("BET", BET, V2[0:1, 0:D], "betdma"), ("FBI", FBI, V2[0:1, D:2 * D], "fbdma")]
        for bi, (nm, dst, row, dmae) in enumerate(bco):
            if nm == "BET":
                ev["betdma"] = Q.op(lambda q: q.dma_start(out=V2[0:1, 0:D], in_=BETd),
                                    waits=[("T", ev["bcWSC"])], inc=16)
            if nm == "FBI":
                ev["fbdma"] = Q.op(lambda q: q.dma_start(out=V2[0:1, D:2 * D], in_=FBd),
                                   waits=[("T", ev["bcGAM"])], inc=16)
            for h in range(2):
                w_ = []
                if h == 0:
                    w_.append(("Q", ev[dmae]))
                    if bi == 0:
                        w_ += [("V", ev["negws"]), ("S", ev["fqev7"])]
                    else:
                        w_.append(("S", ev[f"bcev{bi - 1}"]))
                ev[f"bc{nm}"] = T.op(
                    (lambda rr, hh: lambda t: t.matmul(
                        pb[6][:, 0:512] if hh == 0 else pb[7][:, 0:512],
                        ONE1[:], rr[0:1, hh * 512:(hh + 1) * 512],
                        start=True, stop=True))(row, h),
                    waits=w_, inc=1 if h == 1 else 0)

            def bcev(dd):
                def go(s):
                    s.copy(dd[:, 0:512], pb[6][:, 0:512])
                    return s.copy(dd[:, 512:1024], pb[7][:, 0:512])
                return go
            ev[f"bcev{bi}"] = S.op(bcev(dst), waits=[("T", ev[f"bc{nm}"])], inc=1)
        ev["prepS"] = ev["bcev3"]
        ev["prepV"] = ev["negws"]


    NTILES = NT_WARM + NT_MAIN

    def p2(col):
        return ST[:, col:col + 2]

    def newton_rsqrt_thunks(vcol, outcol, n_iter=6, waits0=()):
        th = [(180, (lambda a, b, w: lambda: V.op(
                  (lambda: lambda v: v.tensor_scalar(p2(b), p2(a), 1.0, 0.5, OP.add, OP.mult))(),
                  reads=(f"P{a}",), writes=("P62",), waits=list(w)))(vcol, 62, waits0)),
              (180, (lambda b: lambda: V.op(
                  lambda v: v.reciprocal(p2(b), p2(62)),
                  reads=("P62",), writes=(f"P{b}",)))(outcol))]
        for _ in range(n_iter):
            th += [
                (180, (lambda b: lambda: V.op(
                    (lambda: lambda v: v.tensor_tensor(p2(62), p2(b), p2(b), OP.mult))(),
                    reads=(f"P{b}",), writes=("P62",)))(outcol)),
                (180, (lambda a: lambda: V.op(
                    (lambda: lambda v: v.scalar_tensor_tensor(
                        p2(62), p2(62), -0.5, p2(a), OP.mult, OP.mult))(),
                    reads=("P62", f"P{a}"), writes=("P62",)))(vcol)),
                (180, (lambda b: lambda: V.op(
                    (lambda: lambda v: v.scalar_tensor_tensor(
                        p2(b), p2(62), 1.5, p2(b), OP.add, OP.mult))(),
                    reads=("P62", f"P{b}"), writes=(f"P{b}",)))(outcol))]
        return th

    def newton_rsqrt(vcol, outcol, n_iter=6, waits0=()):
        for _, t in newton_rsqrt_thunks(vcol, outcol, n_iter, waits0):
            t()

    def emit_scan(i):
        warm = i < NT_WARM
        wsc = [("S", ev[f"buev{i}"])]
        if f"hsb{i - 1}" in ev:
            wsc.append(("S", ev[f"hsb{i - 1}"]))
        ev[f"scan{i}"] = V.op(
            lambda v: v.tensor_tensor_scan(BU[:], DEC[:], BU[:], HC[:], OP.mult, OP.add),
            reads=("DEC", "BU", "HC"), writes=("BU",),
            waits=wsc, inc=1)
        ev[f"hcev{i}"] = V.op(lambda v: v.tensor_copy(HC[:], BU[:, TT - 1:TT]),
                              reads=("BU", "HC"), writes=("HC",), inc=1)
        if i == NT_WARM - 1:
            V.op(lambda v: v.tensor_tensor(HC[:], HC[:], FLG[:], OP.mult),
                 reads=("HC", "FLG"), writes=("HC",),
                 waits=[("Q", ev["fdma"])])
        if not warm:
            wh = [("V", ev[f"scan{i}"])]
            if f"y{i - 1}_{MI - 1}" in ev:
                wh.append(("T", ev[f"y{i - 1}_{MI - 1}"]))
            ev[f"hsb{i}"] = S.op(
                lambda s: s.copy(HSB[:], BU[:]),
                reads=(), writes=("HSB",),
                waits=wh, inc=1)

    def emit_xdma(j):
        if j >= NTILES:
            return
        pj = j % 2
        for c in range(NCH):
            r0 = j * TT + c * 128
            w_ = []
            if c == 0:
                if j >= 2:
                    w_ += [("V", ev[f"xtfreeV{j - 2}"]), ("S", ev[f"xtfreeS{j - 2}"])]
            ev[f"xdma{j}"] = Q.op(
                (lambda pp, cc, rr: lambda q: q.dma_start(
                    out=XT[pp][:, cc * D:(cc + 1) * D], in_=Xd[rr:rr + 128, :]))(pj, c, r0),
                waits=w_, inc=16)

    def stats_S_pre(i):
        p = i % 2
        for c in range(NCH):
            wsq = [("Q", ev[f"xdma{i}"])]
            if c == 0 and i == 0:
                wsq.append(("V", ev["diag"]))
            if c == 0 and i >= 1:
                wsq.append(("T", ev[f"fm1T{i - 1}_{NCH - 1}"]))
            if c == 0 and f"sq3{i - 3}_{NCH - 1}" in ev:
                wsq.append(("V", ev[f"sq3{i - 3}_{NCH - 1}"]))
            ev[f"sx{i}_{c}"] = S.op(
                (lambda pp, cc: lambda s: s.activation(
                    usq(cc), XT[pp][:, cc * D:(cc + 1) * D],
                    AFT.Identity, accum_out=st(0 + cc)))(p, c),
                reads=("XNDMP",), writes=("XNDMP", f"ST0{c}"),
                waits=wsq, inc=1)
            ev[f"sxx{i}_{c}"] = S.op(
                (lambda pp, cc: lambda s: s.activation(
                    usq(cc), XT[pp][:, cc * D:(cc + 1) * D],
                    AFT.Square, accum_out=st(2 + cc)))(p, c),
                reads=("XNDMP",), writes=("XNDMP", f"ST2{c}"),
                inc=1)

    def stats_V_thunks(i):
        p = i % 2
        EP = 12 + 2 * (i % 2)
        SG = 8 + 2 * (i % 2)
        th = []
        for c in range(NCH):
            th.append((200, (lambda cc: lambda: V.op(
                (lambda: lambda v: v.tensor_scalar(st(4 + cc), st(0 + cc), 1.0 / D, None, OP.mult))(),
                reads=(f"ST0{cc}",), writes=(f"ST4{cc}",),
                waits=[("S", ev[f"sx{i}_{cc}"])]))(c)))
            th.append((200, (lambda cc: lambda: V.op(
                (lambda: lambda v: v.tensor_scalar(st(38 + cc), st(2 + cc), 1.0 / D, None, OP.mult))(),
                reads=(f"ST2{cc}",), writes=(f"ST38{cc}",),
                waits=[("S", ev[f"sxx{i}_{cc}"])]))(c)))
            th.append((200, (lambda cc: lambda: V.op(
                (lambda: lambda v: v.tensor_tensor(st(42 + cc), st(4 + cc), st(4 + cc), OP.mult))(),
                reads=(f"ST4{cc}",), writes=(f"ST42{cc}",)))(c)))
            th.append((200, (lambda cc: lambda: V.op(
                (lambda: lambda v: v.tensor_tensor(st(38 + cc), st(38 + cc), st(42 + cc), OP.subtract))(),
                reads=(f"ST38{cc}", f"ST42{cc}"), writes=(f"ST38{cc}",)))(c)))
            th.append((200, (lambda cc: lambda: ev.__setitem__(f"vpe1{i}_{cc}", V.op(
                (lambda: lambda v: v.tensor_scalar(st(40 + cc), st(38 + cc), st(44), None, OP.add))(),
                reads=(f"ST38{cc}", "ST44"), writes=(f"ST40{cc}", "P40"), inc=1)))(c)))

        def nwt():
            w0 = [("V", ev[f"vpe1{i}_{c}"]) for c in range(NCH)]
            if f"xn1{i - 1}_{NCH - 1}" in ev:
                w0.append(("S", ev[f"xn1{i - 1}_{NCH - 1}"]))
            return newton_rsqrt_thunks(40, 6, waits0=w0)
        nbox = []
        def nstep(k):
            def go():
                if not nbox:
                    nbox.append(nwt())
                nbox[0][k][1]()
            return go
        for k in range(2 + 6 * 3):
            th.append((180, nstep(k)))
        th.append((200, (lambda sg: lambda: V.op(
            (lambda: lambda v: v.tensor_tensor(p2(sg), p2(40), p2(6), OP.mult))(),
            reads=("ST400", "ST401", "P6"),
            writes=(f"ST{sg}0", f"ST{sg}1")))(SG)))
        th.append((200, lambda: V.op(
            lambda v: v.tensor_tensor(p2(60), p2(4), p2(6), OP.mult),
            reads=("ST40", "ST41", "P6"), writes=("ST600", "ST601"))))
        th.append((200, lambda: V.op(
            lambda v: v.tensor_scalar(p2(60), p2(60), -1.0, None, OP.mult),
            reads=("ST600", "ST601"), writes=("ST600", "ST601"))))
        th.append((200, (lambda ee: lambda: ev.__setitem__(f"cf1{i}", V.op(
            (lambda: lambda v: v.tensor_scalar(p2(ee), p2(40), LN_EPS, None, OP.mult))(),
            reads=("ST400", "ST401"),
            writes=(f"ST{ee}0", f"ST{ee}1", f"EPP{ee}"), inc=1)))(EP)))
        return th

    def stats_S_post(i):
        p = i % 2
        for c in range(NCH):
            w_ = [("V", ev[f"cf1{i}"])]
            if c == 0 and f"zlast{i - 1}" in ev:
                w_.append(("V", ev[f"zlast{i - 1}"]))
            ev[f"xn1{i}_{c}"] = S.op(
                (lambda pp, cc: lambda s: s.activation(
                    XN1B[:, cc * D:(cc + 1) * D], XT[pp][:, cc * D:(cc + 1) * D],
                    AFT.Identity, bias=st(60 + cc), scale=st(6 + cc)))(p, c),
                reads=(), writes=(f"XN1B_sq{c}",),
                waits=w_, inc=1)
        ev[f"xtfreeS{i}"] = ev[f"xn1{i}_{NCH - 1}"]

    def mixer_stats(i):
        if i == 0:
            emit_xdma(0)
        stats_S_pre(i)
        for _, t in stats_V_thunks(i):
            t()
        stats_S_post(i)

    def mixer_main(i):
        p = i % 2
        warm = i < NT_WARM
        if i == 0:
            emit_xdma(1)
        FM1A = FM1[:].rearrange("p (k t) -> p k t", k=KD)
        for c in range(NCH):
            vb = vb4 if c == 0 else vb5
            for k in range(KD):
                w_ = []
                if k == 0:
                    w_.append(("S", ev[f"xn1{i}_{c}"]))
                    if f"fm2ev{i - 2}_{c}" in ev:
                        w_.append(("S", ev[f"fm2ev{i - 2}_{c}"]))
                    if i == 0 and c == 0:
                        w_.append(("S", ev["wbev"]))
                    if i == NT_WARM and c == 0:
                        w_.append(("S", ev["fqev7"]))
                    if f"o1{i - 2}_1_{c}" in ev:
                        w_.append(("V", ev[f"o1{i - 2}_1_{c}"]))
                ev[f"fm1T{i}_{c}"] = T.op(
                    (lambda cc, kk, vv: lambda t: t.transpose(
                        vv[:, kk * 128:(kk + 1) * 128],
                        XN1B[:, cc * D + kk * 128:cc * D + (kk + 1) * 128], IDT[:]))(c, k, vb),
                    waits=w_, inc=1 if k == KD - 1 else 0)
            ev[f"fm1ev{i}_{c}"] = S.op(
                (lambda cc, vv: lambda s: s.copy(
                    FM1A[:, 0:KD, cc * 128:(cc + 1) * 128], vv[:, 0:1024]))(c, vb),
                reads=(), writes=(f"FM1_{c}",),
                waits=[("T", ev[f"fm1T{i}_{c}"])], inc=1)

        WB_BANKS = (0, 1, 7)
        for f in range(MI + 4):
          if 2 <= f < MI + 2:
            fc = f - 2
            cbank = pb[2 + fc % 2]
            xloc = (fc % 2) * XSLOT
            for kk in range(4):
                w3 = []
                if kk == 0:
                    if i == 0 and fc == 0:
                        w3.append(("V", ev["diag"]))
                    w3.append(("S", ev[f"xinev{i}_{fc}"]))
                    if fc >= 2:
                        w3.append(("S", ev[f"uev{i}_{fc - 2}"]))
                    elif i >= 1:
                        w3.append(("S", ev[f"uev{i - 1}_{MI - 2 + fc}"]))
                    if fc < 2 and f"v2{i - 1}_1" in ev:
                        w3.append(("V", ev[f"v2{i - 1}_1"]))
                ev[f"cv{i}_{fc}"] = T.op(
                    (lambda ff, k2, cb, lo: lambda t: t.matmul(
                        cb[:, 0:TT], Dcv[:, (ff * 4 + k2) * 128:(ff * 4 + k2 + 1) * 128],
                        XIN[:, lo + k2:lo + k2 + TT],
                        start=(k2 == 0), stop=(k2 == 3)))(fc, kk, cbank, xloc),
                    waits=w3, inc=1 if kk == 3 else 0)
          if f < MI:
            bank = pb[WB_BANKS[f % 3]]
            w_ = []
            if f == 0:
                w_.append(("S", ev[f"fm1ev{i}_{NCH - 1}"]))
            if f < 2 and f"o1{i - 2}_0_{f}" in ev:
                w_.append(("V", ev[f"o1{i - 2}_0_{f}"]))
            if f == 2 and i == NT_WARM:
                w_.append(("S", ev["prepS"]))
            w_.append(("S", ev[f"wev{f}"]))
            if f >= 3:
                w_.append(("S", ev[f"xinev{i}_{f - 3}"]))
            elif i >= 1:
                w_.append(("S", ev[f"xinev{i - 1}_{MI - 1}"]))
            if f == 2 and f"v2{i - 1}_0" in ev:
                w_.append(("V", ev[f"v2{i - 1}_0"]))
            for k in range(KD):
                ev[f"xp{i}_{f}"] = T.op(
                    (lambda ff, kk, bb: lambda t: t.matmul(
                        bb[:, 0:TT], Wi[:, (ff * KD + kk) * 128:(ff * KD + kk + 1) * 128],
                        FM1[:, kk * TT:(kk + 1) * TT],
                        start=(kk == 0), stop=(kk == KD - 1)))(f, k, bank),
                    waits=w_ if k == 0 else [], inc=1 if k == KD - 1 else 0)
          if f >= 4:
            fb = f - 4
            w4 = [("S", ev[f"uev{i}_{fb}"])]
            if fb == 0:
                w4.append(("S", ev["wbev"]))
                if i == NT_WARM:
                    w4.append(("S", ev["prepS"]))
                if f"v2{i - 1}_0" in ev:
                    w4.append(("V", ev[f"v2{i - 1}_0"]))
                if f"buev{i - 1}" in ev:
                    w4.append(("S", ev[f"buev{i - 1}"]))
            ev[f"bu{i}_{fb}"] = T.op(
                (lambda ff: lambda t: t.matmul(
                    pb[6][0:STATE, 0:TT], Wb[:, ff * STATE:(ff + 1) * STATE],
                    U[:, ff * TT:(ff + 1) * TT],
                    start=(ff == 0), stop=(ff == MI - 1)))(fb),
                waits=w4, inc=1 if fb == MI - 1 else 0)
          if f < MI:
            xlo = (f % 2) * XSLOT
            w2 = [("T", ev[f"xp{i}_{f}"])]
            if f >= 2:
                w2.append(("T", ev[f"cv{i}_{f - 2}"]))
            elif i >= 1:
                w2.append(("T", ev[f"cv{i - 1}_{MI - 2 + f}"]))
            if i == 0 and f < 2:
                w2.append(("V", ev["hlomem"]))

            S.op((lambda ff, lo: lambda s: s.copy(
                    XIN[:, lo:lo + HALO], HLO[:, ff * 4:ff * 4 + HALO]))(f, xlo),
                 reads=(f"HLO{f}",), writes=(f"XINh{f % 2}",),
                 waits=w2[1:] if f >= 2 or i >= 1 else w2)
            S.op((lambda lo, bb: lambda s: s.copy(
                    XIN[:, lo + HALO:lo + HALO + TT], bb[:, 0:TT]))(xlo, bank),
                 reads=(), writes=(f"XIN{f % 2}",),
                 waits=[("T", ev[f"xp{i}_{f}"])])
            ev[f"xinev{i}_{f}"] = S.op(
                (lambda ff, lo: lambda s: s.copy(
                    HLO[:, ff * 4:ff * 4 + HALO], XIN[:, lo + TT:lo + TT + HALO]))(f, xlo),
                reads=(f"XIN{f % 2}",), writes=(f"HLO{f}",), inc=1)
          if f == 8 and warm and i + 1 < NTILES_ALL:
            mixer_stats(i + 1)
          if 2 <= f < MI + 2:
            fu = f - 2
            cbank = pb[2 + fu % 2]
            wu = [("T", ev[f"cv{i}_{fu}"]), ("Q", ev["cbdma"])]
            if i == 0 and fu == 0:
                wu.append(("S", ev[f"wcast{MI - 1}"]))
            if fu == 8 and f"sq2{i - 1}_{NCH - 1}" in ev:
                wu.append(("V", ev[f"sq2{i - 1}_{NCH - 1}"]))
            ev[f"uev{i}_{fu}"] = S.op(
                (lambda ff, cb: lambda s: s.activation(
                    U[:, ff * TT:(ff + 1) * TT], cb[:, 0:TT], AFT.Silu,
                    bias=CB[:, ff:ff + 1]))(fu, cbank),
                reads=(), writes=(f"U{fu}",),
                waits=wu, inc=1)
        wbu = [("T", ev[f"bu{i}_{MI - 1}"])]
        if f"hcev{i - 1}" in ev:
            wbu.append(("V", ev[f"hcev{i - 1}"]))
        if f"hsb{i - 1}" in ev:
            wbu.append(("S", ev[f"hsb{i - 1}"]))
        ev[f"buev{i}"] = S.op(
            lambda s: s.activation(BU[:], pb[6][0:STATE, 0:TT], AFT.Copy, scale=0.05),
            reads=(), writes=("BU",),
            waits=wbu, inc=1)
        if warm:
            emit_scan(i)
            ev[f"xtfreeV{i}"] = ev["hlomem"]
            emit_xdma(i + 2)

    def gate_block(i):
        p = i % 2
        emit_scan(i)
        sth = qth_def.pop(i - 1, []) + lnq_def.pop(i - 1, [])

        def xpg(f):
            bank = pb[f % 2]
            w_ = [("S", ev[f"wev{MI + f}"])]
            if f < 2:
                w_.append(("S", ev[f"xinev{i}_{MI - 1}"]))
            else:
                w_.append(("S", ev[f"gev{i}_{f - 2}"]))
            for k in range(KD):
                ev[f"xpg{i}_{f}"] = T.op(
                    (lambda ff, kk, bb: lambda t: t.matmul(
                        bb[:, 0:TT],
                        Wi[:, ((MI + ff) * KD + kk) * 128:((MI + ff) * KD + kk + 1) * 128],
                        FM1[:, kk * TT:(kk + 1) * TT],
                        start=(kk == 0), stop=(kk == KD - 1)))(f, k, bank),
                    waits=w_ if k == 0 else [], inc=1 if k == KD - 1 else 0)

        def gev(f):
            bank = pb[f % 2]
            wg = [("T", ev[f"xpg{i}_{f}"])]
            if f >= 8:
                wg.append(("V", ev[f"z{i}_{f - 8}"]))
            if f == 0:
                wg.append(("V", ev[f"epi{i - 2}_1"] if f"epi{i - 2}_1" in ev
                           else ev["prepV"]))
            ev[f"gev{i}_{f}"] = S.op(
                (lambda ff, bb: lambda s: s.activation(
                    G[:, (ff % 8) * TT:(ff % 8 + 1) * TT], bb[:, 0:TT], AFT.Silu))(f, bank),
                reads=("G",), writes=("G",),
                waits=wg, inc=1)

        xpg(0); gev(0); xpg(1); gev(1)
        if i + 1 < NTILES_ALL:
            stats_S_pre(i + 1)
            sth = sth + stats_V_thunks(i + 1)
        for f in range(MI):
            ybank = YB[f % 2]
            w_ = [("S", ev[f"gev{i}_{f}"])]
            if f == 0:
                w_ += [("S", ev[f"hsb{i}"]), ("S", ev["wcev1"])]
                w_.append(("S", ev[f"fm1ev{i}_0"]))
            if f == 1:
                w_.append(("S", ev[f"fm1ev{i}_1"]))
            if f >= 2:
                w_.append(("V", ev[f"zt{i}_{f - 2}"]))
            ev[f"y{i}_{f}"] = T.op(
                (lambda ff, yb: lambda t: t.matmul(
                    yb[:, 0:TT], Wc[0:STATE, ff * 128:(ff + 1) * 128], HSB[:],
                    start=True, stop=True))(f, yb=ybank),
                waits=w_, inc=1)
            w5 = [("T", ev[f"y{i}_{f}"]), ("S", ev[f"gev{i}_{f}"])]
            if i == NT_WARM and f == 0:
                w5.append(("Q", ev["dpdma"]))
            if f == 0:
                w5.append(("T", ev[f"fm1T{i}_{NCH - 1}"]))
            ev[f"zt{i}_{f}"] = V.op(
                (lambda ff, yb: lambda v: v.scalar_tensor_tensor(
                    ZT[:, 0:TT], U[:, ff * TT:(ff + 1) * TT], DP[:, ff:ff + 1],
                    yb[:, 0:TT], OP.mult, OP.add))(f, ybank),
                reads=("ZT",), writes=("ZT",),
                waits=w5, inc=1)
            wz = []
            if f >= 2:
                wz.append(("T", ev[f"wo{i}_{f - 2}"]))
            ev[f"z{i}_{f}"] = V.op(
                (lambda ff: lambda v: v.tensor_tensor(
                    Z[:, (ff % 2) * TT:(ff % 2 + 1) * TT], ZT[:, 0:TT],
                    G[:, (ff % 8) * TT:(ff % 8 + 1) * TT], OP.mult))(f),
                reads=("ZT", "Z"), writes=("Z",), waits=wz, inc=1)
            if f + 2 < MI:
                xpg(f + 2); gev(f + 2)
            if f >= 1:
                budget = 1500
                while sth:
                    c0, t = sth[0]
                    if budget < 1500 and c0 > budget:
                        break
                    t()
                    budget -= c0
                    sth = sth[1:]
            w6 = [("V", ev[f"z{i}_{f}"])]
            if f == 0:
                if i == NT_WARM:
                    w6.append(("S", ev["prepS"]))
                w6.append(("T", ev[f"cv{i}_{MI - 1}"]))
                w6.append(("S", ev[f"uev{i}_{MI - 1}"]))
                w6.append(("S", ev[f"buev{i}"]))
                if f"v2{i - 1}_0" in ev:
                    w6.append(("V", ev[f"v2{i - 1}_0"]))
            for c in range(NCH):
                banks = (pb[6], pb[7]) if c == 0 else (pb[2], pb[3])
                for h in range(2):
                    ev[f"wo{i}_{f}"] = T.op(
                        (lambda ff, cc, hh, bk: lambda t: t.matmul(
                            bk[:, 0:512],
                            Z[:, (ff % 2) * TT + cc * 128:(ff % 2) * TT + (cc + 1) * 128],
                            Wo[:, ff * D + hh * 512:ff * D + (hh + 1) * 512],
                            start=(ff == 0), stop=(ff == MI - 1)))(f, c, h, banks[h]),
                        waits=w6 if (c == 0 and h == 0) else [],
                        inc=1 if (c == NCH - 1 and h == 1) else 0)
        for _, t in sth:
            t()
        ev[f"zlast{i}"] = ev[f"z{i}_{MI - 1}"]

    def v2_block(i):
        p = i % 2
        for c in range(NCH):
            banks = (pb[6], pb[7]) if c == 0 else (pb[2], pb[3])
            for h in range(2):
                w_ = [("T", ev[f"wo{i}_{MI - 1}"])]
                if c == 0 and h == 0:
                    if f"odma{i - 1}" in ev:
                        w_.append(("G", ev[f"odma{i - 1}"]))
                    if f"odmaq{i - 1}" in ev:
                        w_.append(("Q", ev[f"odmaq{i - 1}"]))
                    if i == NT_WARM:
                        w_.append(("S", ev["prepS"]))
                ev[f"v2{i}_{c}"] = V.op(
                    (lambda cc, hh, bk, pp, sg: lambda v: v.scalar_tensor_tensor(
                        V2[:, cc * D + hh * 512:cc * D + (hh + 1) * 512],
                        bk[:, 0:512], st(sg + cc),
                        XT[pp][:, cc * D + hh * 512:cc * D + (hh + 1) * 512],
                        OP.mult, OP.add,
                        accum_out=st(46 + 2 * cc + hh)))(c, h, banks[h], p, 8 + 2 * (i % 2)),
                    reads=(f"ST{8 + 2 * (i % 2)}{c}", f"XT{i % 2}_{c}"),
                    writes=(f"V2_{c}_{h}", f"ST46_{2 * c + h}"),
                    waits=w_, inc=1)

    def ln_block(i):
        p = i % 2
        EP = 12 + 2 * (i % 2)

        for c in range(NCH):
            w_ = [("V", ev[f"v2{i}_{c}"])]
            if i == NT_WARM and c == 0:
                w_.append(("S", ev["prepS"]))
            if c == 0:
                w_.append(("V", ev[f"zlast{i}"]))
            ev[f"sq2{i}_{c}"] = V.op(
                (lambda cc: lambda v: v.scalar_tensor_tensor(
                    SQD[:], V2[:, cc * D:(cc + 1) * D], 1.0,
                    V2[:, cc * D:(cc + 1) * D], OP.mult, OP.mult,
                    accum_out=st(54 + cc)))(c),
                reads=(f"V2_{c}_0", f"V2_{c}_1", "SQD"),
                writes=("SQD", "P54"),
                waits=w_, inc=1)
            V.op((lambda cc: lambda v: v.tensor_tensor(
                st(58 + cc), st(46 + 2 * cc), st(47 + 2 * cc), OP.add))(c),
                 reads=(f"ST46_{2 * c}", f"ST46_{2 * c + 1}"), writes=("P58",))
        V.op(lambda v: v.tensor_scalar(p2(58), p2(58), 1.0 / D, None, OP.mult),
             reads=("P58",), writes=("P58",))
        V.op(lambda v: v.tensor_scalar(p2(54), p2(54), 1.0 / D, None, OP.mult),
             reads=("P54",), writes=("P54",))
        V.op(lambda v: v.tensor_tensor(p2(38), p2(58), p2(58), OP.mult),
             reads=("P58",), writes=("P38",))
        V.op(lambda v: v.tensor_tensor(p2(54), p2(54), p2(38), OP.subtract),
             reads=("P54", "P38"), writes=("P54",))
        V.op((lambda ee: lambda v: v.tensor_tensor(p2(54), p2(54), p2(ee), OP.add))(EP),
             reads=("P54", f"EPP{EP}"), writes=("P54",))
        newton_rsqrt(54, 16)
        V.op(lambda v: v.tensor_tensor(p2(18), p2(58), p2(16), OP.mult),
             reads=("P58", "P16"), writes=("P18",))
        ev[f"cf2{i}"] = V.op(
            lambda v: v.tensor_scalar(p2(18), p2(18), -1.0, None, OP.mult),
            reads=("P18",), writes=("P18",), inc=1)
        for c in range(NCH):
            ev[f"n2{i}_{c}"] = V.op(
                (lambda cc: lambda v: v.tensor_scalar(
                    V2[:, cc * D:(cc + 1) * D], V2[:, cc * D:(cc + 1) * D],
                    st(16 + cc), st(18 + cc), OP.mult, OP.add))(c),
                reads=("P16", "P18", f"V2_{c}_0", f"V2_{c}_1"),
                writes=(f"V2_{c}_0", f"V2_{c}_1"), inc=1)
            ev[f"t{i}_{c}"] = V.op(
                (lambda cc: lambda v: v.tensor_tensor(
                    V2[:, cc * D:(cc + 1) * D], V2[:, cc * D:(cc + 1) * D], GAM[:], OP.mult))(c),
                reads=(f"V2_{c}_0", f"V2_{c}_1", "GAM"), writes=(f"V2_{c}_0", f"V2_{c}_1"), inc=1)
            ev[f"xB{i}_{c}"] = V.op(
                (lambda pp, cc: lambda v: v.tensor_tensor(
                    XT[pp][:, cc * D:(cc + 1) * D], XT[pp][:, cc * D:(cc + 1) * D],
                    BET[:], OP.add))(p, c),
                reads=("BET", f"XT{p}_{c}"), writes=(f"XT{p}_{c}",), inc=1)
            ev[f"r1{i}_{c}"] = V.op(
                (lambda pp, cc: lambda v: v.scalar_tensor_tensor(
                    V2[:, cc * D:(cc + 1) * D], V2[:, cc * D:(cc + 1) * D], 1.0,
                    XT[pp][:, cc * D:(cc + 1) * D], OP.mult, OP.add,
                    accum_out=st(52 + cc)))(p, c),
                reads=(f"V2_{c}_0", f"V2_{c}_1", f"XT{p}_{c}"),
                writes=(f"V2_{c}_0", f"V2_{c}_1", "P52"), inc=1)
        ev[f"xtfreeV{i}"] = ev[f"r1{i}_{NCH - 1}"]
        emit_xdma(i + 2)
        for c in range(NCH):
            ev[f"rmx{i}_{c}"] = V.op(
                (lambda cc: lambda v: v.reduce_max(
                    st(24 + cc), V2[:, cc * D:(cc + 1) * D], axis=AX.X))(c),
                reads=(f"V2_{c}_0", f"V2_{c}_1"), writes=("P24",),
                waits=[("V", ev[f"r1{i}_{c}"])], inc=1)
            ev[f"rmn{i}_{c}"] = V.op(
                (lambda cc: lambda v: v.tensor_reduce(
                    st(26 + cc), V2[:, cc * D:(cc + 1) * D], AX.X, OP.min))(c),
                reads=(f"V2_{c}_0", f"V2_{c}_1"), writes=("P26",), inc=1)
        V.op(lambda v: v.tensor_scalar(p2(20), p2(52), 1.0 / D, None, OP.mult),
             reads=("P52",), writes=("P20",))
        V.op(lambda v: v.tensor_tensor(p2(38), p2(24), p2(20), OP.subtract),
             reads=("P24", "P20"), writes=("P38",))
        V.op(lambda v: v.tensor_tensor(p2(40), p2(20), p2(26), OP.subtract),
             reads=("P20", "P26"), writes=("P40",))
        V.op(lambda v: v.tensor_tensor(p2(38), p2(38), p2(40), OP.max),
             reads=("P38", "P40"), writes=("P38",))
        V.op(lambda v: v.tensor_scalar(p2(28), p2(38), 1e-12, None, OP.max),
             reads=("P38",), writes=("P28",))
        V.op(lambda v: v.reciprocal(p2(32), p2(28)),
             reads=("P28",), writes=("P32",))
        V.op(lambda v: v.tensor_scalar(p2(34), p2(32), 127.0, None, OP.mult),
             reads=("P32",), writes=("P34",))
        V.op(lambda v: v.tensor_tensor(p2(36), p2(20), p2(34), OP.mult),
             reads=("P20", "P34"), writes=("P36",))
        ev[f"AB{i}"] = V.op(
            lambda v: v.tensor_scalar(p2(36), p2(36), -1.0, None, OP.mult),
            reads=("P36",), writes=("P36",), inc=1)
        lnq = []
        for c in range(NCH):
            lnq.append((1200, (lambda cc: lambda: ev.__setitem__(f"sq3{i}_{cc}", V.op(
                (lambda: lambda v: v.scalar_tensor_tensor(
                    XND[:, D:2 * D], V2[:, cc * D:(cc + 1) * D], 1.0,
                    V2[:, cc * D:(cc + 1) * D], OP.mult, OP.mult,
                    accum_out=st(56 + cc)))(),
                reads=(f"V2_{cc}_0", f"V2_{cc}_1", "XNDMP"),
                writes=("XNDMP", "P56"),
                waits=([("S", ev[f"sxx{i + 2}_{NCH - 1}"])]
                       if (cc == 0 and f"sxx{i + 2}_{NCH - 1}" in ev) else []),
                inc=1)))(c)))
        lnq.append((220, lambda: V.op(
            lambda v: v.tensor_scalar(p2(56), p2(56), 1.0 / D, None, OP.mult),
            reads=("P56",), writes=("P56",))))
        lnq.append((220, lambda: V.op(
            lambda v: v.tensor_tensor(p2(62), p2(20), p2(20), OP.mult),
            reads=("P20",), writes=("P62",))))
        lnq.append((220, lambda: V.op(
            lambda v: v.tensor_tensor(p2(56), p2(56), p2(62), OP.subtract),
            reads=("P56", "P62"), writes=("P56",))))
        lnq.append((220, lambda: V.op(
            lambda v: v.tensor_scalar(p2(56), p2(56), st(44), None, OP.add),
            reads=("P56", "ST44"), writes=("P56",))))
        lnq += newton_rsqrt_thunks(56, 22)
        lnq.append((220, lambda: V.op(
            lambda v: v.tensor_tensor(p2(30), p2(28), p2(22), OP.mult),
            reads=("P28", "P22"), writes=("P30",))))
        lnq.append((220, lambda: V.op(
            lambda v: v.tensor_scalar(p2(30), p2(30), -1.0 / 127.0, None, OP.mult),
            reads=("P30",), writes=("P30", "ST300", "ST301"))))
        for c in range(NCH):
            lnq.append((1250, (lambda cc: lambda: ev.__setitem__(f"rB{i}_{cc}", V.op(
                (lambda: lambda v: v.tensor_tensor(
                    V2[:, cc * D:(cc + 1) * D], V2[:, cc * D:(cc + 1) * D],
                    FBI[:], OP.add))(),
                reads=(f"V2_{cc}_0", f"V2_{cc}_1", "FBI"),
                writes=(f"V2_{cc}_0", f"V2_{cc}_1"),
                waits=[("V", ev[f"tq{i}_{NCH - 1}"])] if cc == 0 else [],
                inc=1)))(c)))
        lnq_def[i] = lnq
        qth = []
        for c in range(NCH):
            qth.append((900, (lambda cc: lambda: ev.__setitem__(f"tq{i}_{cc}", V.op(
                (lambda: lambda v: v.tensor_scalar(
                    TQS[:], V2[:, cc * D:(cc + 1) * D],
                    st(34 + cc), st(36 + cc), OP.mult, OP.add))(),
                reads=("P34", "P36", "TQS", f"V2_{cc}_0", f"V2_{cc}_1"),
                writes=("TQS",),
                waits=([("V", ev[f"epi{i - 1}_1"] if f"epi{i - 1}_1" in ev
                        else ev["prepV"])] if cc == 0 else []),
                inc=1)))(c)))
            qth.append((750, (lambda cc: lambda: ev.__setitem__(f"qk{i}_{cc}", V.op(
                (lambda: lambda v: v.tensor_copy(I8Q[:, 0:D], TQS[:]))(),
                reads=("TQS", "I8Q"), writes=("I8Q",),
                waits=([("T", ev[f"fm1T{i + 1}_{NCH - 1}"])]
                       if (cc == 0 and f"fm1T{i + 1}_{NCH - 1}" in ev) else []),
                inc=1)))(c)))
            qth.append((750, (lambda cc: lambda: ev.__setitem__(f"qk2{i}_{cc}", V.op(
                (lambda: lambda v: v.tensor_copy(
                    XQB[:, cc * D:(cc + 1) * D], I8Q[:, 0:D]))(),
                reads=("I8Q",), writes=(f"XQB{cc}",), inc=1)))(c)))
            qth.append((1300, (lambda cc: lambda: ev.__setitem__(f"qr0{i}_{cc}", V.op(
                (lambda: lambda v: v.tensor_tensor(
                    TQS[:], TQS[:], XQB[:, cc * D:(cc + 1) * D], OP.subtract))(),
                reads=("TQS", f"XQB{cc}"), writes=("TQS",), inc=1)))(c)))
            qth.append((1300, (lambda cc: lambda: V.op(
                (lambda: lambda v: v.scalar_tensor_tensor(
                    XQB[:, cc * D:(cc + 1) * D], TQS[:], 0.5,
                    XQB[:, cc * D:(cc + 1) * D], OP.is_ge, OP.add))(),
                reads=("TQS", f"XQB{cc}"), writes=(f"XQB{cc}",)))(c)))
            qth.append((1300, (lambda cc: lambda: ev.__setitem__(f"qr{i}_{cc}", V.op(
                (lambda: lambda v: v.scalar_tensor_tensor(
                    XQB[:, cc * D:(cc + 1) * D], TQS[:], -0.5,
                    XQB[:, cc * D:(cc + 1) * D], OP.is_le, OP.subtract))(),
                reads=("TQS", f"XQB{cc}"), writes=(f"XQB{cc}",), inc=1)))(c)))
        qth_def[i] = qth

    def ffn_block(i, last=False):
        FM2A = FM2[:].rearrange("p (k t) -> p k t", k=KD)
        for c in range(NCH):
            vb = vb4 if c == 0 else vb5
            for k in range(KD):
                w_ = []
                if k == 0:
                    w_.append(("V", ev[f"qr{i}_{NCH - 1}"]))
                    w_.append(("S", ev[f"fm1ev{i + 1}_{c}"] if f"fm1ev{i + 1}_{c}" in ev else ev[f"fm1ev{i}_{c}"]))
                    if f"zt{i + 1}_{MI - 1}" in ev:
                        w_.append(("V", ev[f"zt{i + 1}_{MI - 1}"]))
                ev[f"fm2T{i}_{c}"] = T.op(
                    (lambda cc, kk, vv: lambda t: t.transpose(
                        vv[:, kk * 128:(kk + 1) * 128],
                        XQB[:, cc * D + kk * 128:cc * D + (kk + 1) * 128], IDT[:]))(c, k, vb),
                    waits=w_, inc=1 if k == KD - 1 else 0)
            ev[f"fm2ev{i}_{c}"] = S.op(
                (lambda cc, vv: lambda s: s.copy(
                    FM2A[:, 0:KD, cc * 128:(cc + 1) * 128], vv[:, 0:1024]))(c, vb),
                reads=(), writes=(f"FM2_{c}",),
                waits=[("T", ev[f"fm2T{i}_{c}"])], inc=1)
        for c in range(NCH):
            for h in range(2):
                bank = pb[h] if c == 0 else FB[h]
                w_ = []
                if c == 0 and h == 0:
                    w_.append(("S", ev[f"fm2ev{i}_{NCH - 1}"]))
                    if f"gev{i + 1}_{MI - 1}" in ev:
                        w_.append(("S", ev[f"gev{i + 1}_{MI - 1}"]))
                    elif f"o1{i - 1}_0_1" in ev:
                        w_.append(("V", ev[f"o1{i - 1}_0_1"]))
                if c == 1 and h == 0:
                    w_.append(("S", ev[f"fm2ev{i}_0"]))
                if c == 1 and h == 1:
                    w_.append(("S", ev[f"fm2ev{i}_1"]))
                for k in range(KD):
                    ev[f"ffn{i}_{c}_{h}"] = T.op(
                        (lambda cc, hh, kk, bk: lambda t: t.matmul(
                            bk[:, 0:512],
                            FM2[:, kk * TT + cc * 128:kk * TT + (cc + 1) * 128],
                            Wq[:, kk * D + hh * 512:kk * D + (hh + 1) * 512],
                            start=(kk == 0), stop=(kk == KD - 1)))(c, h, k, bank),
                        waits=w_ if k == 0 else [], inc=1 if k == KD - 1 else 0)
            for h in range(2):
                bank = pb[h] if c == 0 else FB[h]
                wo1 = [("T", ev[f"ffn{i}_{c}_{h}"]), ("V", ev[f"rB{i}_{c}"])]
                if c == 0 and h == 0:
                    wo1.append(("V", ev[f"zlast{i + 1}"] if f"zlast{i + 1}" in ev
                                else ev[f"zlast{i}"]))
                ev[f"o1{i}_{c}_{h}"] = V.op(
                    (lambda cc, hh, bk: lambda v: v.scalar_tensor_tensor(
                        SCR[:, hh * 512:(hh + 1) * 512], bk[:, 0:512], st(30 + cc),
                        WSC[:, hh * 512:(hh + 1) * 512], OP.mult, OP.mult))(c, h, bank),
                    reads=(f"ST30{c}", "WSC"), writes=(f"SCR{h}",),
                    waits=wo1, inc=1)
            ev[f"epi{i}_{c}"] = V.op(
                (lambda cc: lambda v: v.tensor_tensor(
                    V2[:, cc * D:(cc + 1) * D], SCR[:],
                    V2[:, cc * D:(cc + 1) * D], OP.add))(c),
                reads=("SCR0", "SCR1", f"V2_{c}_0", f"V2_{c}_1"),
                writes=(f"V2_{c}_0", f"V2_{c}_1"), inc=1)
            r0 = (i - NT_WARM) * TT + c * 128
            if c == 0:
                ev[f"odma{i}"] = Gq.op(
                    (lambda cc, rr: lambda g: g.dma_start(
                        out=OUTd[rr:rr + 128, :], in_=V2[:, cc * D:(cc + 1) * D]))(c, r0),
                    waits=[("V", ev[f"epi{i}_{c}"])], inc=16)
            else:
                ev[f"odmaq{i}"] = Q.op(
                    (lambda cc, rr: lambda q: q.dma_start(
                        out=OUTd[rr:rr + 128, :], in_=V2[:, cc * D:(cc + 1) * D]))(c, r0),
                    waits=[("V", ev[f"epi{i}_{c}"])], inc=16)

    NTILES_ALL = NT_WARM + NT_MAIN
    mixer_stats(0)
    for i in range(NTILES_ALL):
        if i == NT_WARM:
            prep_late()
        if i - 1 >= NT_WARM:
            ln_block(i - 1)
        mixer_main(i)
        if i >= NT_WARM:
            gate_block(i)
        if i - 1 >= NT_WARM:
            ffn_block(i - 1)
        if i >= NT_WARM and i + 1 < NTILES_ALL:
            stats_S_post(i + 1)
        if i >= NT_WARM:
            v2_block(i)
    ln_block(NTILES_ALL - 1)
    for _, t in qth_def.pop(NTILES_ALL - 1):
        t()
    for _, t in lnq_def.pop(NTILES_ALL - 1):
        t()
    ffn_block(NTILES_ALL - 1, last=True)

    total_odma = 16 * NT_MAIN
    total_q = Q.cnt
    if debug:
        dbg = {}
        for nm, buf in (("DBG_FM1", FM1), ("DBG_FM2", FM2), ("DBG_BU", BU),
                        ("DBG_HSB", HSB), ("DBG_XQB", XQB), ("DBG_XB", XT[(NTILES_ALL - 1) % 2]),
                        ("DBG_ST", ST), ("DBG_HC", HC), ("DBG_V2", V2),
                        ("DBG_WQ", Wq), ("DBG_WSC", WSC), ("DBG_GAM", GAM),
                        ("DBG_FBI", FBI), ("DBG_BET", BET)):
            shp = [buf.shape[0], buf.shape[1]] if hasattr(buf, 'shape') else None
            d = nc.dram_tensor(nm, list(buf.shape), buf.dtype, kind="ExternalOutput").ap()
            dbg[nm] = (d, buf)
        nd = 0
        for nm, (d, buf) in dbg.items():
            Gq.op((lambda dd, bb: lambda g: g.dma_start(out=dd, in_=bb[:]))(d, buf),
                  waits=[("G", total_odma)] if nd == 0 else [], inc=16)
            nd += 1
        total_odma += 16 * nd
    Gq.op(lambda g: g.nop(), waits=[("G", total_odma), ("Q", total_q)])

    with es:
        with nc.Block() as block:
            @block.sync
            def _(sync):
                for it in Q.items:
                    it(sync, sems)

            @block.tensor
            def _(t):
                for it in T.items:
                    it(t, sems)

            @block.vector
            def _(v):
                for it in V.items:
                    it(v, sems)

            @block.scalar
            def _(s):
                for it in S.items:
                    it(s, sems)

            @block.gpsimd
            def _(g):
                for it in Gq.items:
                    it(g, sems)

    return nc


def kernel(L0_attn_x, L0_attn_W_in, L0_attn_conv_w, L0_attn_conv_b,
           L0_attn_W_B, L0_attn_W_C, L0_attn_D_param, L0_attn_W_out,
           L0_attn_ln_gamma, L0_attn_ln_beta, L0_ffn_weight, L0_ffn_bias):
    x = np.ascontiguousarray(np.asarray(L0_attn_x, np.float32))
    B, Tn, Dn = x.shape

    conv_w = np.asarray(L0_attn_conv_w, np.float32).reshape(INNER, 4)
    CWh = np.ascontiguousarray(conv_w.reshape(MI, 128, 4).transpose(1, 0, 2).reshape(128, MI * 4))
    CBh = np.ascontiguousarray(np.asarray(L0_attn_conv_b, np.float32).reshape(MI, 128).T)
    DPh = np.ascontiguousarray(np.asarray(L0_attn_D_param, np.float32).reshape(MI, 128).T)
    GAMh = np.ascontiguousarray(np.asarray(L0_attn_ln_gamma, np.float32).reshape(1, D))
    BETh = np.ascontiguousarray(np.asarray(L0_attn_ln_beta, np.float32).reshape(1, D))
    FBh = np.ascontiguousarray(np.asarray(L0_ffn_bias, np.float32).reshape(1, D))
    IDTh = np.eye(128, dtype=ml_dtypes.bfloat16)
    WINh = np.ascontiguousarray(np.asarray(L0_attn_W_in, np.float32))
    WOUTh = np.ascontiguousarray(np.asarray(L0_attn_W_out, np.float32))
    WBh = np.ascontiguousarray(np.asarray(L0_attn_W_B, np.float32))
    WCh = np.ascontiguousarray(np.asarray(L0_attn_W_C, np.float32))
    FWh = np.ascontiguousarray(np.asarray(L0_ffn_weight, np.float32))

    in_maps = []
    for c in range(8):
        b, h = c // 2, c % 2
        main = x[b, h * NTOK:(h + 1) * NTOK]
        if h == 0:
            warm = np.zeros((WARM, D), np.float32)
        else:
            warm = x[b, NTOK - WARM:NTOK]
        X = np.ascontiguousarray(np.concatenate([warm, main], axis=0))
        FLGh = np.full((STATE, 1), float(h), np.float32)
        in_maps.append({
            "X": X, "WIN": WINh, "WOUT": WOUTh, "WB": WBh, "WC": WCh,
            "FW": FWh, "CW": CWh, "CB": CBh, "DP": DPh, "GAM": GAMh,
            "BET": BETh, "FB": FBh, "IDT": IDTh, "FLG": FLGh,
        })

    nc = build_nc()
    res = run_bass_kernel_spmd(nc, in_maps, core_ids=list(range(8)))
    out = np.empty((B, Tn, Dn), np.float32)
    for c in range(8):
        b, h = c // 2, c % 2
        out[b, h * NTOK:(h + 1) * NTOK] = res.results[c]["OUT"]
    return out
